# version 21
# baseline (speedup 1.0000x reference)
import numpy as np
import ml_dtypes
from contextlib import ExitStack

import concourse.bass as bass
import concourse.mybir as mybir
from concourse.bass_utils import run_bass_kernel_spmd

F32 = mybir.dt.float32
BF16 = mybir.dt.bfloat16
FP8 = mybir.dt.float8e4
U8 = mybir.dt.uint8
AF = mybir.ActivationFunctionType
ALU = mybir.AluOpType
AX = mybir.AxisListType

NCORES = 8
SEQS = 2
T = 2048
D = 1024
H = 16
HD = 64
ROPE = 32
LAT = 128
NT = T // 128
QB = 256
NQB = T // QB
NG = 4
EPS_LN = 1e-5
EPS_RMS = 1e-6


def chunk_entries():
    out = []
    for qb in range(NQB):
        for g in range(NG):
            nch = 2 * qb + 2
            for c in range(nch):
                out.append(dict(qb=qb, g=g, c=c,
                                qoff=(128 if c == 2 * qb + 1 else 0),
                                first=(c == 0), last=(c == nch - 1),
                                diag=(c >= 2 * qb)))
    return out


ENTRIES = chunk_entries()
LAST_G, LAST_QB = ENTRIES[-1]["g"], ENTRIES[-1]["qb"]


class Sched:
    def __init__(self):
        self.counts = {}
        self.marks = {}

    def inc(self, sem, n):
        self.counts[sem] = self.counts.get(sem, 0) + n

    def mark(self, sem, key):
        self.marks[(sem, key)] = self.counts.get(sem, 0)

    def val(self, sem, key):
        return self.marks.get((sem, key), 0)


class Eng:
    def __init__(self, sched, handle, sems, dry):
        self.s = sched
        self.h = handle
        self.sems = sems
        self.dry = dry
        self.gates = []

    def push_gate(self, b):
        self.gates.append(b)

    def pop_gate(self):
        self.gates.pop()

    def do(self, fn, inc=None, n=1, mark=None):
        if self.gates and not all(self.gates):
            return
        if self.dry:
            if inc is not None:
                self.s.inc(inc, n)
                if mark is not None:
                    self.s.mark(inc, mark)
            return
        instr = fn()
        if inc is not None:
            instr.then_inc(self.sems[inc], n)

    def dobar(self, fn):
        self.svb_count = getattr(self, "svb_count", 0) + 1
        self.do(fn, inc="svb", n=1)

    def waitbar(self):
        if self.gates and not all(self.gates):
            return
        if self.dry:
            return
        if getattr(self, "svb_count", 0) > 0:
            self.h.wait_ge(self.sems["svb"], self.svb_count)

    def wait(self, sem, key):
        if self.gates and not all(self.gates):
            return
        if self.dry:
            return
        v = self.s.val(sem, key)
        if v > 0:
            self.h.wait_ge(self.sems[sem], v)


def build_core(phases=99):
    pass_seqs = SEQS if phases >= 4 else 1

    nc = bass.Bass("TRN2", target_bir_lowering=False,
                   detect_race_conditions=False)

    x_in = nc.dram_tensor("x", [SEQS, T, D], F32, kind="ExternalInput").ap()
    wq_in = nc.dram_tensor("wq", [D, D], F32, kind="ExternalInput").ap()
    wkv_in = nc.dram_tensor("wkv", [D, LAT], F32, kind="ExternalInput").ap()
    wk_in = nc.dram_tensor("wk", [LAT, 512], F32, kind="ExternalInput").ap()
    wv_in = nc.dram_tensor("wv", [LAT, 512], F32, kind="ExternalInput").ap()
    wo_in = nc.dram_tensor("wo", [D, D], F32, kind="ExternalInput").ap()
    qg_in = nc.dram_tensor("qg", [1, HD], F32, kind="ExternalInput").ap()
    kg_in = nc.dram_tensor("kg", [1, ROPE], F32, kind="ExternalInput").ap()
    w8_in = nc.dram_tensor("w8", [D, D], U8, kind="ExternalInput").ap()
    ws_in = nc.dram_tensor("ws", [1, 1], F32, kind="ExternalInput").ap()
    fb_in = nc.dram_tensor("fb", [1, D], F32, kind="ExternalInput").ap()
    out = nc.dram_tensor("out", [SEQS, T, D], F32, kind="ExternalOutput").ap()

    sched = Sched()
    do_p3 = phases >= 2
    do_p4 = phases >= 3
    nseq = SEQS if phases >= 4 else (1 if phases < 4 else SEQS)


    with ExitStack() as st:
        e = st.enter_context

        wq_bf = e(nc.sbuf_tensor("wq_bf", [128, 8 * 1024], BF16))
        wkv_bf = e(nc.sbuf_tensor("wkv_bf", [128, 8 * 128], BF16))
        wk_bf = e(nc.sbuf_tensor("wk_bf", [128, 512], BF16))
        wv_bf = e(nc.sbuf_tensor("wv_bf", [128, 512], BF16))
        wo_bf = e(nc.sbuf_tensor("wo_bf", [128, 4 * 1024], BF16))
        wt_bf = e(nc.sbuf_tensor("wt_bf", [128, 8 * 1024], BF16))
        bias_bc = e(nc.sbuf_tensor("bias_bc", [128, 1024], F32))
        qg8_bc = e(nc.sbuf_tensor("qg8_bc", [128, 512], BF16))
        kg_bc = e(nc.sbuf_tensor("kg_bc", [128, 512], BF16))
        ident = e(nc.sbuf_tensor("ident", [128, 128], BF16))
        tri = e(nc.sbuf_tensor("tri", [128, 128], BF16))
        ones32 = e(nc.sbuf_tensor("ones32", [128, 32], BF16))
        onesrow = e(nc.sbuf_tensor("onesrow", [1, 128], F32))
        ubws = e(nc.sbuf_tensor("ubws", [128, 2], F32))
        ubias = ubws[:, 0:1]
        ws128 = ubws[:, 1:2]
        qg_sb = e(nc.sbuf_tensor("qg_sb", [1, HD], F32))
        kg_sb = e(nc.sbuf_tensor("kg_sb", [1, ROPE], F32))
        ws_sb = e(nc.sbuf_tensor("ws_sb", [1, 1], F32))
        fb_sb = e(nc.sbuf_tensor("fb_sb", [1, D], F32))
        sm_sb = e(nc.sbuf_tensor("sm_sb", [1, 8], F32))
        eps_sb = e(nc.sbuf_tensor("eps_sb", [128, 1], F32))
        zrow = e(nc.sbuf_tensor("zrow", [128, 256], BF16))


        qhT = e(nc.sbuf_tensor("qhT", [128, NG * T], BF16))
        khT = e(nc.sbuf_tensor("khT", [128, NG * T], BF16))
        vtok = e(nc.sbuf_tensor("vtok", [128, NT * 512], BF16))
        otf = e(nc.sbuf_tensor("otf", [128, NG * T], BF16))
        pt_ring = e(nc.sbuf_tensor("pt_ring", [128, 4 * 1024], BF16))

        xf = e(nc.sbuf_tensor("xf", [128, 2 * 1024], F32))
        xb = e(nc.sbuf_tensor("xb", [128, 2 * 1024], F32))
        ln1 = e(nc.sbuf_tensor("ln1", [128, 2 * 1024], BF16))
        lnT = e(nc.sbuf_tensor("lnT", [128, 2 * 1024], BF16))
        qf = e(nc.sbuf_tensor("qf", [128, 1024], F32))
        qsq = e(nc.sbuf_tensor("qsq", [128, 1024], F32))
        qhat = e(nc.sbuf_tensor("qhat", [128, 512], BF16))
        khat = e(nc.sbuf_tensor("khat", [128, 512], BF16))
        kvsb = e(nc.sbuf_tensor("kvsb", [128, 128], BF16))
        kvT = e(nc.sbuf_tensor("kvT", [128, 128], BF16))
        rtmp = e(nc.sbuf_tensor("rtmp", [128, 512], F32))
        stats = e(nc.sbuf_tensor("stats", [128, 128], F32))
        rq = e(nc.sbuf_tensor("rq", [128, 32], F32))
        sums_g = e(nc.sbuf_tensor("sums_g", [128, 256], F32))
        sums_r = e(nc.sbuf_tensor("sums_r", [128, 256], F32))
        aosb = e(nc.sbuf_tensor("aosb", [128, 1024], F32))
        xmid = e(nc.sbuf_tensor("xmid", [128, 2 * 1024], F32))
        xadj = e(nc.sbuf_tensor("xadj", [128, 1024], F32))
        qin8 = e(nc.sbuf_tensor("qin8", [128, 1024], FP8))
        abf = e(nc.sbuf_tensor("abf", [128, 2 * 1024], BF16))
        atT = e(nc.sbuf_tensor("atT", [128, 2 * 1024], BF16))
        ffo = e(nc.sbuf_tensor("ffo", [128, 1024], F32))
        xout = e(nc.sbuf_tensor("xout", [128, 2 * 1024], F32))

        pS = e(nc.psum_tensor("pS", [128, 2048], F32))
        pA = pS[:, 0:1024]
        pB = pS[:, 1024:2048]
        pC = e(nc.psum_tensor("pC", [128, 512], F32))
        pT = e(nc.psum_tensor("pT", [128, 1024], BF16))
        pW = e(nc.psum_tensor("pW", [128, 512], F32))
        pT2 = pW.bitcast(BF16)
        pTf = pT.bitcast(F32)
        pD = e(nc.psum_tensor("pD", [128, 512], F32))
        pDb = pD.bitcast(BF16)

        xmid_u8 = xmid[:, :].bitcast(U8)
        dec_u = xmid_u8[:, 0:1024]
        dec_a = xmid_u8[:, 1024:2048]
        dec_b = xmid_u8[:, 2048:3072]
        dec_sg = xmid_u8[:, 3072:4096]
        dec_m = xmid_u8[:, 4096:5120]
        dec_A = abf[:, 0:1024]
        dec_B = abf[:, 1024:2048]
        wn_bf = atT
        w8_st = xout[:, :].bitcast(U8)
        rowbuf = qsq

        sems = {n: e(nc.semaphore(n)) for n in
                ["svb", "stb", "st", "sv", "ss", "sg", "ssm", "swq0", "swq1", "swo0",
                 "swo1", "skv", "swk", "swv", "sw80", "sw81", "sx0", "sx1",
                 "sb0", "sb1", "so0", "so1"]}
        block = e(nc.Block())


        def stream_sync(E):
            E.do(lambda: E.h.dma_start(qg_sb[:, :], qg_in), inc="ssm", n=16)
            E.do(lambda: E.h.dma_start(kg_sb[:, :], kg_in), inc="ssm", n=16)
            E.do(lambda: E.h.dma_start(ws_sb[:, :], ws_in), inc="ssm", n=16)
            E.do(lambda: E.h.dma_start(fb_sb[:, :], fb_in), inc="ssm", n=16,
                 mark="smalls")
            for g in range(8):
                E.wait("sv", ("wqcast", g - 2))
                E.do(lambda g=g: E.h.dma_start(
                    xf[:, (g % 2) * 1024:(g % 2 + 1) * 1024],
                    wq_in[g * 128:(g + 1) * 128, :]), inc="swq%d" % (g % 2),
                    n=16, mark=("wq", g))
            for g in range(4):
                E.wait("sv", ("wocast", g - 2))
                src = bass.AP(wo_in.tensor, (4 * g) * 64 * 1024,
                              [[64 * 1024, 4], [1024, 32], [1, 1024]])
                E.do(lambda g=g, src=src: E.h.dma_start(
                    xb[:, (g % 2) * 1024:(g % 2 + 1) * 1024], src),
                    inc="swo%d" % (g % 2), n=16, mark=("wo", g))
            E.wait("sv", ("wqcast", 7))
            E.do(lambda: E.h.dma_start(
                xf[:, 0:1024].rearrange("p (a b) -> p a b", a=8),
                wkv_in.rearrange("(c p) l -> p c l", p=128)), inc="skv", n=16,
                mark="wkv")
            E.wait("sv", ("wocast", 3))
            E.do(lambda: E.h.dma_start(xb[:, 0:512], wk_in), inc="swk", n=16,
                 mark="wk")
            E.do(lambda: E.h.dma_start(xb[:, 512:1024], wv_in), inc="swv", n=16,
                 mark="wv")
            for j in range(8):
                E.wait("sv", ("decsel", j - 2))
                E.do(lambda j=j: E.h.dma_start(
                    w8_st[:, (j % 2) * 1024:(j % 2 + 1) * 1024],
                    w8_in[j * 128:(j + 1) * 128, :]), inc="sw8%d" % (j % 2),
                    n=16, mark=("w8", j))
            for s in range(SEQS):
                for t in range(NT):
                    if s == 0 and t < 2:
                        E.wait("sv", "wkvcast")
                    else:
                        E.wait("ss", ("ln1", s, t - 2) if t >= 2 else
                               ("ln1", s - 1, NT + t - 2))
                    E.do(lambda s=s, t=t: E.h.dma_start(
                        xf[:, (t % 2) * 1024:(t % 2 + 1) * 1024],
                        x_in[s, t * 128:(t + 1) * 128, :]),
                        inc="sx%d" % (t % 2), n=16, mark=("x", s, t))
                for t in range(NT):
                    if s == 0 and t < 2:
                        E.wait("sv", "wkwvcast")
                    else:
                        E.wait("sv", ("ln2res", s, t - 2) if t >= 2 else
                               ("ln2res", s - 1, NT + t - 2))
                    E.do(lambda s=s, t=t: E.h.dma_start(
                        xb[:, (t % 2) * 1024:(t % 2 + 1) * 1024],
                        x_in[s, t * 128:(t + 1) * 128, :]),
                        inc="sb%d" % (t % 2), n=16, mark=("xb", s, t))

        def stream_gpsimd(E):
            E.do(lambda: E.h.memset(ident[:, :], 0.0), inc="sg", n=1,
                 mark="c_mset")
            E.do(lambda: E.h.memset(tri[:, :], 1.0))
            E.do(lambda: E.h.memset(ones32[:, :], 1.0))
            E.do(lambda: E.h.memset(onesrow[:, :], 1.0))
            E.do(lambda: E.h.memset(eps_sb[:, :], EPS_LN))
            E.do(lambda: E.h.memset(zrow[:, :], 0.0), inc="sg", n=1,
                 mark="c_mset2")
            E.wait("sg", "c_mset2")
            E.do(lambda: E.h.affine_select(
                out=ident[:, :], in_=ident[:, :], compare_op=ALU.not_equal,
                fill=1.0, base=0, pattern=[[-1, 128]], channel_multiplier=1))
            E.do(lambda: E.h.affine_select(
                out=tri[:, :], in_=tri[:, :], compare_op=ALU.is_ge,
                fill=0.0, base=0, pattern=[[1, 128]], channel_multiplier=-1),
                inc="sg", n=1, mark="consts")
            for s in range(SEQS):
                for t in range(NT):
                    E.wait("sv", ("xout", s, t))
                    E.do(lambda s=s, t=t: E.h.dma_start(
                        out[s, t * 128:(t + 1) * 128, :],
                        xout[:, (t % 2) * 1024:(t % 2 + 1) * 1024]),
                        inc="so%d" % (t % 2), n=16, mark=("out", s, t))
            E.wait("so0", ("out", SEQS - 1, NT - 2))
            E.wait("so1", ("out", SEQS - 1, NT - 1))

        def combine_ln(E, src, dst, inv_n, inc=None, mark=None):
            alu = ALU
            E.do(lambda: E.h.bn_aggr(stats[:, 12:14], stats[:, src:src + 6]))
            E.dobar(lambda: E.h.bn_aggr(stats[:, 14:16], stats[:, src + 6:src + 12]))
            E.waitbar()
            E.do(lambda: E.h.tensor_scalar(
                stats[:, dst:dst + 1], stats[:, 12:13], stats[:, 14:15], 0.5,
                alu.add, alu.mult))
            E.dobar(lambda: E.h.tensor_scalar(
                stats[:, 124:125], stats[:, 12:13], stats[:, 14:15], None,
                alu.subtract))
            E.waitbar()
            E.do(lambda: E.h.tensor_scalar(
                stats[:, 125:126], stats[:, 124:125], stats[:, 124:125], 0.25,
                alu.mult, alu.mult))
            E.dobar(lambda: E.h.tensor_scalar(
                stats[:, dst + 1:dst + 2], stats[:, 13:14], stats[:, 15:16], 0.5,
                alu.add, alu.mult))
            E.waitbar()
            E.do(lambda: E.h.tensor_scalar(
                stats[:, dst + 1:dst + 2], stats[:, dst + 1:dst + 2],
                stats[:, 125:126], None, alu.add), inc=inc, n=1, mark=mark)

        def stream_vector(E):
            alu = ALU
            for g in range(8):
                E.wait("swq%d" % (g % 2), ("wq", g))
                E.do(lambda g=g: E.h.tensor_copy(
                    wq_bf[:, g * 1024:(g + 1) * 1024],
                    xf[:, (g % 2) * 1024:(g % 2 + 1) * 1024]),
                    inc="sv", n=1, mark=("wqcast", g))
            for g in range(4):
                E.wait("swo%d" % (g % 2), ("wo", g))
                E.do(lambda g=g: E.h.tensor_copy(
                    wo_bf[:, g * 1024:(g + 1) * 1024],
                    xb[:, (g % 2) * 1024:(g % 2 + 1) * 1024]),
                    inc="sv", n=1, mark=("wocast", g))
            E.wait("skv", "wkv")
            E.do(lambda: E.h.tensor_copy(wkv_bf[:, :], xf[:, 0:1024]),
                 inc="sv", n=1, mark="wkvcast")
            E.wait("swk", "wk")
            E.do(lambda: E.h.tensor_copy(wk_bf[:, :], xb[:, 0:512]))
            E.wait("swv", "wv")
            E.do(lambda: E.h.tensor_copy(wv_bf[:, :], xb[:, 512:1024]),
                 inc="sv", n=1, mark="wkwvcast")
            E.wait("ssm", "smalls")
            E.do(lambda: E.h.tensor_scalar(
                rowbuf[0:1, 0:512].rearrange("p (h r) -> p h r", h=16),
                qg_sb[0:1, 0:ROPE].unsqueeze(1).broadcast_to([1, 16, ROPE]),
                0.125, None, alu.mult))
            E.dobar(lambda: E.h.tensor_copy(
                rowbuf[0:1, 512:1024].rearrange("p (h r) -> p h r", h=16),
                kg_sb[0:1, 0:ROPE].unsqueeze(1).broadcast_to([1, 16, ROPE])))
            E.waitbar()
            E.do(lambda: E.h.tensor_reduce(
                sm_sb[0:1, 0:1].unsqueeze(2), rowbuf[0:1, 0:32].unsqueeze(1),
                AX.X, alu.max, apply_absolute_value=True))
            E.dobar(lambda: E.h.tensor_reduce(
                sm_sb[0:1, 1:2].unsqueeze(2), kg_sb[0:1, 0:ROPE].unsqueeze(1),
                AX.X, alu.max, apply_absolute_value=True))
            E.waitbar()
            E.do(lambda: E.h.tensor_scalar(
                sm_sb[0:1, 3:4], sm_sb[0:1, 0:1], sm_sb[0:1, 1:2],
                -22.627416997969522, alu.mult, alu.mult),
                inc="sv", n=1, mark="usmall")
            E.wait("st", "rank1a")
            E.do(lambda: E.h.tensor_copy(ubws[:, :], pC[:, 0:2]),
                 inc="sv", n=1, mark="evac_a")
            E.wait("st", "rank1b")
            E.do(lambda: E.h.tensor_copy(qg8_bc[:, :], pC[:, 0:512]),
                 inc="sv", n=1, mark="evac_b")
            E.wait("st", "rank1c")
            E.do(lambda: E.h.tensor_copy(kg_bc[:, :], pC[:, 0:512]),
                 inc="sv", n=1, mark="evac_c")
            E.wait("st", "rank1d")
            E.do(lambda: E.h.tensor_copy(bias_bc[:, :], pA[:, :]),
                 inc="sv", n=1, mark="evac_d")
            for j in range(8):
                E.wait("sw8%d" % (j % 2), ("w8", j))
                w8c = w8_st[:, (j % 2) * 1024:(j % 2 + 1) * 1024]
                E.do(lambda w8c=w8c: E.h.tensor_scalar(
                    dec_u, w8c, 0x7F, None, alu.bitwise_and))
                E.do(lambda w8c=w8c: E.h.tensor_scalar(
                    dec_sg, w8c, 0x80, None, alu.bitwise_and))
                E.do(lambda: E.h.tensor_scalar(
                    dec_a, dec_u, 0x77, None, alu.min))
                E.do(lambda: E.h.tensor_scalar(
                    dec_b, dec_u, 16, 8, alu.max, alu.subtract))
                E.do(lambda: E.h.tensor_scalar(
                    dec_m, dec_u, 0x78, None, alu.is_ge))
                E.do(lambda: E.h.tensor_tensor(
                    dec_a, dec_a, dec_sg, alu.bitwise_or))
                E.do(lambda: E.h.tensor_tensor(
                    dec_b, dec_b, dec_sg, alu.bitwise_or),
                    inc="sv", n=1, mark=("decbits", j))
                E.wait("ss", ("deccast", j))
                if j >= 2:
                    E.wait("st", ("wtT", j - 2))
                E.do(lambda j=j: E.h.select(
                    wn_bf[:, (j % 2) * 1024:(j % 2 + 1) * 1024],
                    dec_m, dec_B, dec_A),
                    inc="sv", n=1, mark=("decsel", j))
                E.wait("st", ("wtT", j))
                E.do(lambda j=j: E.h.tensor_copy(
                    wt_bf[:, :].rearrange("p (g n) -> p g n", g=8)[:, :, j * 128:(j + 1) * 128],
                    pT[:, :].rearrange("p (g n) -> p g n", g=8)),
                    inc="sv", n=1, mark=("wtevac", j))

            for s in range(SEQS):

                def p1A(t):
                    sl = (t % 2) * 1024
                    d = 16 + 40 * (t % 2)
                    xt = xf[:, sl:sl + 1024]
                    E.wait("sx%d" % (t % 2), ("x", s, t))
                    E.do(lambda xt=xt: E.h.bn_stats(stats[:, 0:6], xt[:, 0:512]))
                    E.do(lambda xt=xt: E.h.bn_stats(stats[:, 6:12], xt[:, 512:1024]))
                    combine_ln(E, 0, d, 1.0 / 1024)
                    E.do(lambda d=d: E.h.tensor_scalar(
                        stats[:, d + 3:d + 4], stats[:, d:d + 1], -1.0, None,
                        alu.mult), inc="sv", n=1, mark=("ln1stats", s, t))

                def p1E(t):
                    E.wait("ss", ("qkvfree", s, t))
                    E.do(lambda: E.h.tensor_tensor(qsq[:, :], qf[:, :], qf[:, :],
                                                   alu.mult))
                    E.do(lambda: E.h.tensor_reduce(
                        stats[:, 20:36].unsqueeze(2),
                        qsq[:, :].rearrange("p (h d) -> p h d", h=16), AX.X,
                        alu.add))
                    E.do(lambda: E.h.tensor_tensor(
                        qsq[:, 0:512], xadj[:, 0:512], xadj[:, 0:512], alu.mult))
                    E.dobar(lambda: E.h.tensor_reduce(
                        stats[:, 36:52].unsqueeze(2),
                        qsq[:, 0:512].rearrange("p (h d) -> p h d", h=16),
                        AX.X, alu.add))
                    E.waitbar()
                    E.do(lambda: E.h.tensor_scalar(
                        stats[:, 20:36], stats[:, 20:36], 1.0 / 64, EPS_RMS,
                        alu.mult, alu.add))
                    E.do(lambda: E.h.tensor_scalar(
                        stats[:, 36:52], stats[:, 36:52], 1.0 / 32, EPS_RMS,
                        alu.mult, alu.add),
                        inc="sv", n=1, mark=("rmsstage", s, t))
                    E.wait("ss", ("rmsr", s, t))
                    E.do(lambda: E.h.tensor_tensor(
                        rtmp[:, :].rearrange("p (h r) -> p h r", h=16),
                        qf[:, :].rearrange("p (h d) -> p h d", h=16)[:, :, 0:ROPE],
                        rq[:, 0:16].unsqueeze(2).broadcast_to([128, 16, ROPE]),
                        alu.mult))
                    E.do(lambda: E.h.tensor_tensor(
                        qhat[:, :], rtmp[:, :], qg8_bc[:, :], alu.mult),
                        inc="sv", n=1, mark=("qpost", s, t))
                    E.do(lambda: E.h.tensor_tensor(
                        rtmp[:, :].rearrange("p (h r) -> p h r", h=16),
                        xadj[:, 0:512].rearrange("p (h d) -> p h d", h=16),
                        rq[:, 16:32].unsqueeze(2).broadcast_to([128, 16, ROPE]),
                        alu.mult))
                    E.do(lambda: E.h.tensor_tensor(
                        khat[:, :], rtmp[:, :], kg_bc[:, :], alu.mult),
                        inc="sv", n=1, mark=("kpost", s, t))

                def p1Q(t):
                    E.wait("st", ("qkT", s, t))
                    E.do(lambda t=t: E.h.tensor_copy(
                        qhT[:, :].rearrange("p (g n) -> p g n", g=NG)[:, :, t * 128:(t + 1) * 128],
                        pDb[:, 0:512].rearrange("p (g n) -> p g n", g=NG)))
                    E.do(lambda t=t: E.h.tensor_copy(
                        khT[:, :].rearrange("p (g n) -> p g n", g=NG)[:, :, t * 128:(t + 1) * 128],
                        pDb[:, 512:1024].rearrange("p (g n) -> p g n", g=NG)),
                        inc="sv", n=1, mark=("qkTevac", s, t))

                for u in range(NT + 2):
                    if u == 0:
                        p1A(0)
                    if u + 1 < NT:
                        p1A(u + 1)
                    if 2 <= u <= NT + 1:
                        p1E(u - 2)
                        p1Q(u - 2)
                del p1A, p1E, p1Q
                E.do(lambda: E.h.tensor_copy(pC[:, 0:256], zrow[:, :]))
                E.do(lambda: E.h.tensor_copy(pD[:, 0:256], zrow[:, :]),
                     inc="sv", n=1, mark=("pvz", s))

                def p3e(i, en):
                    slot = i % 4
                    if en["diag"]:
                        E.wait("ss", ("exp", s, i))
                        coff = en["qoff"]
                        pv = pt_ring[:, slot * 1024:(slot + 1) * 1024]
                        E.do(lambda pv=pv, coff=coff: E.h.tensor_tensor(
                            pv.rearrange("p (h q) -> p h q", h=4)[:, :, coff:coff + 128],
                            pv.rearrange("p (h q) -> p h q", h=4)[:, :, coff:coff + 128],
                            tri[:, :].unsqueeze(1).broadcast_to([128, 4, 128]),
                            alu.mult), inc="sv", n=1, mark=("mask", s, i))

                def p3l(i, en):
                    if en["last"]:
                        E.wait("st", ("pv", s, i))
                        g, qb = en["g"], en["qb"]
                        E.dobar(lambda: E.h.tensor_scalar(
                            sums_g[:, :], pD[:, 0:256], 1e-35, None, alu.max))
                        E.waitbar()
                        E.dobar(lambda: E.h.reciprocal(sums_r[:, :],
                                                       sums_g[:, :]))
                        E.waitbar()
                        E.do(lambda g=g, qb=qb: E.h.tensor_tensor(
                            otf[:, g * T + qb * QB: g * T + (qb + 1) * QB],
                            pC[:, 0:256], sums_r[:, :], alu.mult))
                        E.do(lambda: E.h.tensor_copy(pC[:, 0:256], zrow[:, :]))
                        E.do(lambda g=g, qb=qb: E.h.tensor_copy(
                            pD[:, 0:256], zrow[:, :]),
                            inc="sv", n=1, mark=("ot", s, g, qb))


                def vA0(t):
                    E.wait("st", ("womm", s, t, 0))
                    E.do(lambda: E.h.tensor_copy(aosb[:, 0:512], pW[:, :]),
                         inc="sv", n=1, mark=("aoev0", s, t))
                    E.do(lambda: E.h.bn_stats(stats[:, 0:6], aosb[:, 0:512]))

                def vA1(t):
                    d = 60 + 24 * (t % 2)
                    E.wait("st", ("womm", s, t, 1))
                    E.do(lambda: E.h.tensor_copy(aosb[:, 512:1024], pTf[:, :]),
                         inc="sv", n=1, mark=("aoev", s, t))
                    E.do(lambda: E.h.bn_stats(stats[:, 6:12], aosb[:, 512:1024]))
                    combine_ln(E, 0, d, 1.0 / 1024, inc="sv",
                               mark=("ln2stats", s, t))

                def vD(tp):
                    slp = (tp % 2) * 1024
                    E.wait("st", ("atT", s, tp))
                    E.do(lambda slp=slp: E.h.tensor_copy(
                        atT[:, slp:slp + 1024], pT[:, :]),
                        inc="sv", n=1, mark=("atevac", s, tp))

                def vE0(tp):
                    dp = 60 + 24 * (tp % 2)
                    E.wait("st", ("ffnmm", s, tp, 0))
                    E.do(lambda dp=dp: E.h.scalar_tensor_tensor(
                        ffo[:, 0:512], pW[:, :], stats[:, dp + 12:dp + 13],
                        bias_bc[:, 0:512], alu.mult, alu.add),
                        inc="sv", n=1, mark=("ffo0", s, tp))
                    E.do(lambda: E.h.bn_stats(stats[:, 0:6], ffo[:, 0:512]))

                def vE1(tp):
                    dp = 60 + 24 * (tp % 2)
                    E.wait("st", ("ffnmm", s, tp, 1))
                    E.do(lambda dp=dp: E.h.scalar_tensor_tensor(
                        ffo[:, 512:1024], pTf[:, :], stats[:, dp + 12:dp + 13],
                        bias_bc[:, 512:1024], alu.mult, alu.add),
                        inc="sv", n=1, mark=("ffnev", s, tp))
                    E.do(lambda: E.h.bn_stats(stats[:, 6:12], ffo[:, 512:1024]))
                    combine_ln(E, 0, dp + 16, 1.0 / 1024, inc="sv",
                               mark=("ln4stats", s, tp))

                def vB(t):
                    d = 60 + 24 * (t % 2)
                    sl = (t % 2) * 1024
                    xbt = xb[:, sl:sl + 1024]
                    xm = xmid[:, sl:sl + 1024]
                    E.wait("ss", ("rstd2", s, t))
                    E.wait("sb%d" % (t % 2), ("xb", s, t))
                    E.dobar(lambda d=d: E.h.tensor_scalar(
                        stats[:, d + 3:d + 4], stats[:, d:d + 1],
                        stats[:, d + 2:d + 3], -1.0, alu.mult, alu.mult))
                    E.waitbar()
                    E.do(lambda xbt=xbt, d=d: E.h.tensor_scalar(
                        xadj[:, :], xbt, stats[:, d + 3:d + 4], None,
                        alu.add))
                    E.do(lambda sl=sl, d=d: E.h.scalar_tensor_tensor(
                        xmid[:, sl:sl + 1024], aosb[:, :],
                        stats[:, d + 2:d + 3], xadj[:, :], alu.mult,
                        alu.add), inc="sv", n=1, mark=("ln2res", s, t))
                    E.do(lambda xm=xm: E.h.bn_stats(stats[:, 0:6],
                                                    xm[:, 0:512]))
                    E.do(lambda xm=xm: E.h.bn_stats(stats[:, 6:12],
                                                    xm[:, 512:1024]))
                    combine_ln(E, 0, d + 4, 1.0 / 1024)
                    E.do(lambda xm=xm, d=d: E.h.tensor_reduce(
                        stats[:, d + 6:d + 7].unsqueeze(2), xm.unsqueeze(1),
                        AX.X, alu.max))
                    E.do(lambda xm=xm, d=d: E.h.tensor_reduce(
                        stats[:, d + 7:d + 8].unsqueeze(2), xm.unsqueeze(1),
                        AX.X, alu.min), inc="sv", n=1,
                        mark=("ln3stats", s, t))

                def vC(t):
                    d = 60 + 24 * (t % 2)
                    xm = xmid[:, (t % 2) * 1024:(t % 2 + 1) * 1024]
                    E.wait("ss", ("rstd3", s, t))
                    E.do(lambda d=d: E.h.tensor_scalar(
                        stats[:, d + 9:d + 10], stats[:, d + 6:d + 7],
                        stats[:, d + 4:d + 5], None, alu.subtract))
                    E.dobar(lambda d=d: E.h.tensor_scalar(
                        stats[:, d + 10:d + 11], stats[:, d + 4:d + 5],
                        stats[:, d + 7:d + 8], None, alu.subtract))
                    E.waitbar()
                    E.dobar(lambda d=d: E.h.tensor_scalar(
                        stats[:, d + 9:d + 10], stats[:, d + 9:d + 10],
                        stats[:, d + 10:d + 11], None, alu.max))
                    E.waitbar()
                    E.dobar(lambda d=d: E.h.tensor_scalar(
                        stats[:, d + 9:d + 10], stats[:, d + 9:d + 10],
                        stats[:, d + 8:d + 9], 1e-12, alu.mult, alu.max))
                    E.waitbar()
                    E.dobar(lambda d=d: E.h.tensor_scalar(
                        stats[:, d + 11:d + 12], stats[:, d + 9:d + 10],
                        1.0 / 224.0, None, alu.mult))
                    E.waitbar()
                    E.do(lambda d=d: E.h.tensor_scalar(
                        stats[:, d + 10:d + 11], stats[:, d + 11:d + 12],
                        0.0, None, alu.add))
                    E.dobar(lambda d=d: E.h.tensor_scalar(
                        stats[:, d + 12:d + 13], stats[:, d + 11:d + 12],
                        ws128, None, alu.mult))
                    E.waitbar()
                    E.dobar(lambda d=d: E.h.reciprocal(
                        stats[:, d + 13:d + 15], stats[:, d + 10:d + 12]))
                    E.waitbar()
                    E.dobar(lambda d=d: E.h.tensor_scalar(
                        stats[:, d + 13:d + 14], stats[:, d + 8:d + 9],
                        stats[:, d + 14:d + 15], None, alu.mult))
                    E.waitbar()
                    E.dobar(lambda d=d: E.h.tensor_scalar(
                        stats[:, d + 15:d + 16], stats[:, d + 4:d + 5],
                        stats[:, d + 13:d + 14], -1.0, alu.mult, alu.mult))
                    E.waitbar()
                    if t >= 1:
                        E.wait("ss", ("castback", s, t - 1))
                    elif s > 0:
                        E.wait("ss", ("castback", s - 1, NT - 1))
                    E.do(lambda xm=xm, d=d: E.h.tensor_scalar(
                        qin8[:, :], xm, stats[:, d + 13:d + 14],
                        stats[:, d + 15:d + 16], alu.mult, alu.add),
                        inc="sv", n=1, mark=("quant", s, t))

                def vF(tp):
                    dp = 60 + 24 * (tp % 2)
                    slp = (tp % 2) * 1024
                    xmp = xmid[:, slp:slp + 1024]
                    E.wait("ss", ("rstd4", s, tp))
                    E.dobar(lambda dp=dp: E.h.tensor_scalar(
                        stats[:, dp + 19:dp + 20], stats[:, dp + 16:dp + 17],
                        stats[:, dp + 18:dp + 19], -1.0, alu.mult, alu.mult))
                    E.waitbar()
                    E.do(lambda xmp=xmp, dp=dp: E.h.tensor_scalar(
                        xadj[:, :], xmp, stats[:, dp + 19:dp + 20], None,
                        alu.add))
                    if tp >= 2:
                        E.wait("so%d" % (tp % 2), ("out", s, tp - 2))
                    elif s > 0:
                        E.wait("so%d" % (tp % 2), ("out", s - 1, NT + tp - 2))
                    E.do(lambda slp=slp, dp=dp: E.h.scalar_tensor_tensor(
                        xout[:, slp:slp + 1024], ffo[:, :],
                        stats[:, dp + 18:dp + 19], xadj[:, :], alu.mult,
                        alu.add), inc="sv", n=1, mark=("xout", s, tp))

                def p4sched(u0):
                    u1 = u0 + 1
                    out = []

                    def add(slot, fn, u, front):
                        if front and u < NT:
                            out.append((slot, lambda fn=fn, u=u: fn(u)))
                        elif not front and 1 <= u <= NT:
                            out.append((slot, lambda fn=fn, u=u: fn(u - 1)))
                    add(0, vA0, u0, True)
                    add(1, vA1, u0, True)
                    add(2, vD, u0, False)
                    add(3, vE0, u0, False)
                    add(4, vE1, u0, False)
                    add(5, vB, u0, True)
                    add(6, vF, u0, False)
                    add(7, vC, u0, True)
                    add(9, vA0, u1, True)
                    add(10, vA1, u1, True)
                    add(11, vD, u1, False)
                    add(12, vE0, u1, False)
                    add(13, vE1, u1, False)
                    add(14, vB, u1, True)
                    add(15, vF, u1, False)
                    add(16, vC, u1, True)
                    return out

                i = 0
                step = 0
                for qb in range(NQB):
                    pieces = []
                    if qb >= 1:
                        pieces = p4sched(step)
                        step += 2
                    pi = 0
                    k = 0
                    while i < len(ENTRIES) and ENTRIES[i]["qb"] == qb:
                        p3e(i, ENTRIES[i])
                        if i >= 2:
                            p3l(i - 2, ENTRIES[i - 2])
                        i += 1
                        k += 1
                        while pi < len(pieces) and pieces[pi][0] <= k - 2:
                            pieces[pi][1]()
                            pi += 1
                    while pi < len(pieces):
                        pieces[pi][1]()
                        pi += 1
                p3l(len(ENTRIES) - 2, ENTRIES[-2])
                p3l(len(ENTRIES) - 1, ENTRIES[-1])
                for u0 in (NT - 2, NT):
                    for _, fn in p4sched(u0):
                        fn()
                del p3e, p3l, p4sched, vA0, vA1, vB, vC, vD, vE0, vE1, vF

        def stream_scalar(E):
            for j in range(8):
                E.wait("sv", ("decbits", j))
                if j >= 1:
                    E.wait("sv", ("decsel", j - 1))
                E.do(lambda: E.h.activation(dec_A, dec_a.bitcast(FP8),
                                            AF.Copy))
                E.do(lambda: E.h.activation(dec_B, dec_b.bitcast(FP8),
                                            AF.Copy, scale=2.0),
                     inc="ss", n=1, mark=("deccast", j))
            E.wait("sv", "evac_a")
            E.wait("sg", "consts")
            for s in range(SEQS):
                for t in range(NT + 2):
                    if t < NT:
                        d = 16 + 40 * (t % 2)
                        sl = (t % 2) * 1024
                        E.wait("sv", ("ln1stats", s, t))
                        E.do(lambda d=d: E.h.activation(
                            stats[:, d + 2:d + 3], stats[:, d + 1:d + 2],
                            AF.Ln, bias=eps_sb[:, 0:1]))
                        E.do(lambda d=d: E.h.activation(
                            stats[:, d + 2:d + 3], stats[:, d + 2:d + 3],
                            AF.Exp, scale=-0.5),
                             inc="ss", n=1, mark=("rstd1", s, t))
                        E.do(lambda d=d: E.h.activation(
                            stats[:, d + 3:d + 4], stats[:, d + 3:d + 4],
                            AF.Identity, scale=stats[:, d + 2:d + 3]))
                        E.do(lambda d=d, sl=sl: E.h.activation(
                            ln1[:, sl:sl + 1024], xf[:, sl:sl + 1024],
                            AF.Identity, bias=stats[:, d + 3:d + 4],
                            scale=stats[:, d + 2:d + 3]),
                            inc="ss", n=1, mark=("ln1", s, t))
                    if 1 <= t <= NT:
                        tp = t - 1
                        slp = (tp % 2) * 1024
                        trT = pT if tp % 2 == 0 else pT2
                        E.wait("st", ("lnTt", s, tp))
                        E.do(lambda slp=slp, trT=trT: E.h.activation(
                            lnT[:, slp:slp + 1024], trT[:, :], AF.Copy),
                            inc="ss", n=1, mark=("lnTevac", s, tp))
                        E.wait("st", ("kvmm", s, tp))
                        E.do(lambda: E.h.activation(kvT[:, :], pC[:, 256:384],
                                                    AF.Copy),
                             inc="ss", n=1, mark=("kvTevac", s, tp))
                        E.wait("st", ("kvvmm", s, tp))
                    if t >= 2:
                        E.wait("sv", ("rmsstage", s, t - 2))
                        E.do(lambda: E.h.activation(rq[:, :], stats[:, 20:52],
                                                    AF.Ln))
                        E.do(lambda: E.h.activation(rq[:, :], rq[:, :], AF.Exp,
                                                    scale=-0.5),
                             inc="ss", n=1, mark=("rmsr", s, t - 2))
                    if 1 <= t <= NT:
                        if tp >= 1:
                            E.wait("sv", ("kpost", s, tp - 1))
                        elif s > 0:
                            E.wait("sv", ("xout", s - 1, NT - 1))
                        E.do(lambda: E.h.activation(qf[:, :], pA[:, :], AF.Copy))
                        E.do(lambda: E.h.activation(xadj[:, 0:512], pB[:, 0:512],
                                                    AF.Copy))
                        E.do(lambda tp=tp: E.h.activation(
                            vtok[:, tp * 512:(tp + 1) * 512], pB[:, 512:1024],
                            AF.Copy),
                            inc="ss", n=1, mark=("qkvfree", s, tp))
                def p3x(i, en):
                    slot = i % 4
                    qoff = en["qoff"]
                    st_ = (i % 2) * 1024
                    E.wait("st", ("sc", s, i))
                    E.wait("st", ("pv", s, i - 4))
                    pv = pt_ring[:, slot * 1024:(slot + 1) * 1024]
                    src = pS[:, st_:st_ + 1024].rearrange(
                        "p (h q) -> p h q", h=4)[:, :, qoff:256]
                    dst = pv.rearrange("p (h q) -> p h q", h=4)[:, :, qoff:256] \
                        if qoff else pv.rearrange("p (h q) -> p h q", h=4)
                    E.do(lambda src=src, dst=dst: E.h.activation(
                        dst, src, AF.Exp, bias=ubias),
                        inc="ss", n=1, mark=("exp", s, i))


                def x2(t):
                    d = 60 + 24 * (t % 2)
                    E.wait("sv", ("ln2stats", s, t))
                    E.do(lambda d=d: E.h.activation(
                        stats[:, d + 2:d + 3], stats[:, d + 1:d + 2],
                        AF.Ln, bias=eps_sb[:, 0:1]))
                    E.do(lambda d=d: E.h.activation(
                        stats[:, d + 2:d + 3], stats[:, d + 2:d + 3],
                        AF.Exp, scale=-0.5),
                         inc="ss", n=1, mark=("rstd2", s, t))

                def x4(tp):
                    dp = 60 + 24 * (tp % 2)
                    E.wait("sv", ("ln4stats", s, tp))
                    E.do(lambda dp=dp: E.h.activation(
                        stats[:, dp + 18:dp + 19], stats[:, dp + 17:dp + 18],
                        AF.Ln, bias=eps_sb[:, 0:1]))
                    E.do(lambda dp=dp: E.h.activation(
                        stats[:, dp + 18:dp + 19], stats[:, dp + 18:dp + 19],
                        AF.Exp, scale=-0.5),
                         inc="ss", n=1, mark=("rstd4", s, tp))

                def x3(t):
                    d = 60 + 24 * (t % 2)
                    E.wait("sv", ("ln3stats", s, t))
                    E.do(lambda d=d: E.h.activation(
                        stats[:, d + 8:d + 9], stats[:, d + 5:d + 6],
                        AF.Ln, bias=eps_sb[:, 0:1]))
                    E.do(lambda d=d: E.h.activation(
                        stats[:, d + 8:d + 9], stats[:, d + 8:d + 9],
                        AF.Exp, scale=-0.5),
                         inc="ss", n=1, mark=("rstd3", s, t))

                def xcb(t):
                    E.wait("sv", ("quant", s, t))
                    E.wait("st", ("atT", s, t - 2))
                    E.do(lambda t=t: E.h.activation(
                        abf[:, (t % 2) * 1024:(t % 2 + 1) * 1024], qin8[:, :],
                        AF.Copy), inc="ss", n=1, mark=("castback", s, t))

                def p4sched(u0):
                    u1 = u0 + 1
                    out = []

                    def add(slot, fn, u, front):
                        if front and u < NT:
                            out.append((slot, lambda fn=fn, u=u: fn(u)))
                        elif not front and 1 <= u <= NT:
                            out.append((slot, lambda fn=fn, u=u: fn(u - 1)))
                    add(2, x2, u0, True)
                    add(5, x4, u0, False)
                    add(6, x3, u0, True)
                    add(8, xcb, u0, True)
                    add(11, x2, u1, True)
                    add(14, x4, u1, False)
                    add(15, x3, u1, True)
                    add(17, xcb, u1, True)
                    return out

                i = 0
                step = 0
                for qb in range(NQB):
                    pieces = []
                    if qb >= 1:
                        pieces = p4sched(step)
                        step += 2
                    pi = 0
                    k = 0
                    while i < len(ENTRIES) and ENTRIES[i]["qb"] == qb:
                        p3x(i, ENTRIES[i])
                        i += 1
                        k += 1
                        while pi < len(pieces) and pieces[pi][0] <= k - 4:
                            pieces[pi][1]()
                            pi += 1
                    while pi < len(pieces):
                        pieces[pi][1]()
                        pi += 1
                for u0 in (NT - 2, NT):
                    for _, fn in p4sched(u0):
                        fn()
                del p3x, p4sched, x2, x3, x4, xcb

        def stream_tensor(E):
            E.wait("sv", "usmall")
            E.do(lambda: E.h.matmul(pC[:, 0:1], onesrow[:, :], sm_sb[0:1, 3:4],
                                    start=True, stop=True, tile_position=(0, 0)))
            E.do(lambda: E.h.matmul(pC[:, 1:2], onesrow[:, :], ws_sb[0:1, 0:1],
                                    start=True, stop=True, tile_position=(0, 0)),
                 inc="st", n=1, mark="rank1a")
            E.wait("sv", "evac_a")
            E.do(lambda: E.h.matmul(pC[:, 0:512], onesrow[:, :],
                                    rowbuf[0:1, 0:512], start=True, stop=True,
                                    tile_position=(0, 0)),
                 inc="st", n=1, mark="rank1b")
            E.wait("sv", "evac_b")
            E.do(lambda: E.h.matmul(pC[:, 0:512], onesrow[:, :],
                                    rowbuf[0:1, 512:1024], start=True, stop=True,
                                    tile_position=(0, 0)),
                 inc="st", n=1, mark="rank1c")
            E.wait("ssm", "smalls")
            E.do(lambda: E.h.matmul(pA[:, 0:512], onesrow[:, :], fb_sb[0:1, 0:512],
                                    start=True, stop=True, tile_position=(0, 0)))
            E.do(lambda: E.h.matmul(pA[:, 512:1024], onesrow[:, :],
                                    fb_sb[0:1, 512:1024], start=True, stop=True,
                                    tile_position=(0, 0)),
                 inc="st", n=1, mark="rank1d")
            E.wait("sg", "consts")
            for j in range(8):
                E.wait("sv", ("decsel", j))
                if j >= 1:
                    E.wait("sv", ("wtevac", j - 1))
                for g in range(8):
                    E.do(lambda j=j, g=g: E.h.transpose(
                        pT[:, g * 128:(g + 1) * 128],
                        wn_bf[:, (j % 2) * 1024 + g * 128:(j % 2) * 1024 + (g + 1) * 128],
                        ident[:, :]),
                        inc=("st" if g == 7 else None), n=1,
                        mark=(("wtT", j) if g == 7 else None))

            for s in range(SEQS):

                def p2T(t):
                    sl = (t % 2) * 1024
                    trT = pT if t % 2 == 0 else pT2
                    E.wait("ss", ("ln1", s, t))
                    if t >= 2:
                        E.wait("ss", ("lnTevac", s, t - 2))
                    elif s > 0:
                        E.wait("sv", ("ffnev", s - 1, NT - 1) if t == 0
                               else ("ffo0", s - 1, NT - 1))
                    elif t == 0:
                        E.wait("sv", ("wtevac", 7))
                    for g in range(8):
                        E.do(lambda sl=sl, g=g, trT=trT: E.h.transpose(
                            trT[:, g * 128:(g + 1) * 128],
                            ln1[:, sl + g * 128:sl + (g + 1) * 128], ident[:, :]),
                            inc=("st" if g == 7 else None), n=1,
                            mark=(("lnTt", s, t) if g == 7 else None))

                def p2B1(t):
                    sl = (t % 2) * 1024
                    E.wait("ss", ("lnTevac", s, t))
                    if t > 0:
                        E.wait("ss", ("kvTevac", s, t - 1))
                    elif s > 0:
                        E.wait("ss", ("exp", s - 1, len(ENTRIES) - 1))
                        E.wait("sv", ("ot", s - 1, LAST_G, LAST_QB))
                    else:
                        E.wait("sv", "evac_d")
                    for g in range(8):
                        lhs = lnT[:, sl + g * 128:sl + (g + 1) * 128]
                        E.do(lambda g=g, lhs=lhs: E.h.matmul(
                            pC[:, 256:384], wkv_bf[:, g * 128:(g + 1) * 128], lhs,
                            start=(g == 0), stop=(g == 7)),
                            inc=("st" if g == 7 else None), n=1,
                            mark=(("kvmm", s, t) if g == 7 else None))
                    if t > 0:
                        E.wait("ss", ("qkvfree", s, t - 1))
                    for g in range(8):
                        lhs = lnT[:, sl + g * 128:sl + (g + 1) * 128]
                        E.do(lambda g=g, lhs=lhs: E.h.matmul(
                            pA[:, 0:512], lhs, wq_bf[:, g * 1024:g * 1024 + 512],
                            start=(g == 0), stop=(g == 7)))
                        E.do(lambda g=g, lhs=lhs: E.h.matmul(
                            pA[:, 512:1024], lhs,
                            wq_bf[:, g * 1024 + 512:(g + 1) * 1024],
                            start=(g == 0), stop=(g == 7)))

                def p2B2(t):
                    E.wait("ss", ("kvTevac", s, t))
                    E.do(lambda: E.h.matmul(pB[:, 0:512], kvT[:, :], wk_bf[:, :],
                                            start=True, stop=True))
                    E.do(lambda: E.h.matmul(pB[:, 512:1024], kvT[:, :], wv_bf[:, :],
                                            start=True, stop=True),
                         inc="st", n=1, mark=("kvvmm", s, t))

                def p2Q(t):
                    E.wait("sv", ("qpost", s, t))
                    E.wait("sv", ("kpost", s, t))
                    if t >= 1:
                        E.wait("sv", ("qkTevac", s, t - 1))
                    for j in range(4):
                        E.do(lambda j=j: E.h.transpose(
                            pDb[:, j * 128:(j + 1) * 128],
                            qhat[:, j * 128:(j + 1) * 128], ident[:, :]))
                        E.do(lambda j=j: E.h.transpose(
                            pDb[:, 512 + j * 128:512 + (j + 1) * 128],
                            khat[:, j * 128:(j + 1) * 128], ident[:, :]),
                            inc=("st" if j == 3 else None), n=1,
                            mark=(("qkT", s, t) if j == 3 else None))

                for u in range(NT + 2):
                    if u < NT:
                        p2T(u)
                    if 1 <= u <= NT:
                        p2B1(u - 1)
                        p2B2(u - 1)
                    if 2 <= u <= NT + 1:
                        p2Q(u - 2)
                del p2T, p2B1, p2B2, p2Q

                E.wait("sv", ("qkTevac", s, NT - 1))

                def emit_scores(i, en):
                    g, qb, c, qoff = en["g"], en["qb"], en["c"], en["qoff"]
                    st_ = (i % 2) * 1024
                    if i >= 2:
                        E.wait("ss", ("exp", s, i - 2))
                    for hp in (0, 2, 1, 3):
                        lhs = khT[32 * hp:32 * (hp + 1),
                                  g * T + c * 128: g * T + (c + 1) * 128]
                        rhs = qhT[32 * hp:32 * (hp + 1),
                                  g * T + qb * QB + qoff: g * T + (qb + 1) * QB]
                        dst = pS[:, st_ + hp * 256 + qoff:
                                 st_ + (hp + 1) * 256]
                        if hp in (0, 2):
                            E.do(lambda lhs=lhs, rhs=rhs, dst=dst, hp=hp:
                                 E.h.matmul(
                                     dst, lhs, rhs, start=True, stop=True,
                                     tile_position=(32 * hp, 0),
                                     skip_group_check=True),
                                 inc="stb", n=1,
                                 mark=(("scp", s, i) if hp == 2 else None))
                        else:
                            if hp == 1:
                                E.wait("stb", ("scp", s, i))
                            E.do(lambda lhs=lhs, rhs=rhs, dst=dst, hp=hp:
                                 E.h.matmul(
                                     dst, lhs, rhs, start=True, stop=True,
                                     tile_position=(32 * hp, 0),
                                     skip_group_check=True),
                                 inc=("st" if hp == 3 else None), n=1,
                                 mark=(("sc", s, i) if hp == 3 else None))

                def emit_pv(i, en):
                    g, qb, c, qoff = en["g"], en["qb"], en["c"], en["qoff"]
                    slot = i % 4
                    ri = qb * NG + g
                    roff = 0
                    E.wait("ss", ("exp", s, i))
                    if en["diag"]:
                        E.wait("sv", ("mask", s, i))
                    if en["first"]:
                        if ri >= 1:
                            E.wait("sv", ("ot", s, (ri - 1) % NG,
                                          (ri - 1) // NG))
                        else:
                            E.wait("sv", ("pvz", s))
                    pv = pt_ring[:, slot * 1024:(slot + 1) * 1024]
                    for hp in range(4):
                        h = 4 * g + hp
                        vlhs = vtok[:, c * 512 + 32 * h: c * 512 + 32 * (h + 1)]
                        rhs = pv[:, hp * 256 + qoff:(hp + 1) * 256]
                        E.do(lambda vlhs=vlhs, rhs=rhs, hp=hp, en=en, qoff=qoff,
                             roff=roff: E.h.matmul(
                                 pC[32 * hp:32 * (hp + 1),
                                    roff + qoff:roff + 256], vlhs, rhs,
                                 start=False, stop=en["last"],
                                 tile_position=(0, 32 * hp), skip_group_check=True))
                    for hp in range(4):
                        rhs = pv[:, hp * 256 + qoff:(hp + 1) * 256]
                        E.do(lambda rhs=rhs, hp=hp, en=en, qoff=qoff, roff=roff:
                             E.h.matmul(
                                 pD[32 * hp:32 * (hp + 1),
                                    roff + qoff:roff + 256],
                                 ones32[:, :], rhs, start=False, stop=en["last"],
                                 tile_position=(0, 32 * hp), skip_group_check=True),
                             inc=("st" if hp == 3 else None), n=1,
                             mark=(("pv", s, i) if hp == 3 else None))

                def tW0(t):
                    E.wait("sv", ("ot", s, NG - 1, t // 2))
                    if t >= 2:
                        E.wait("sv", ("ffo0", s, t - 2))
                    elif t == 1:
                        E.wait("sv", ("aoev0", s, 0))
                    else:
                        E.wait("sv", ("qkTevac", s, NT - 1))
                    for g in range(4):
                        lhs = otf[:, g * T + t * 128: g * T + (t + 1) * 128]
                        E.do(lambda g=g, lhs=lhs: E.h.matmul(
                            pW[:, 0:512], lhs,
                            wo_bf[:, g * 1024:g * 1024 + 512],
                            start=(g == 0), stop=(g == 3)),
                            inc=("st" if g == 3 else None), n=1,
                            mark=(("womm", s, t, 0) if g == 3 else None))

                def tW1(t):
                    if t >= 2:
                        E.wait("sv", ("ffnev", s, t - 2))
                    elif t == 1:
                        E.wait("sv", ("aoev", s, 0))
                    else:
                        E.wait("sv", ("qkTevac", s, NT - 2))
                    for g in range(4):
                        lhs = otf[:, g * T + t * 128: g * T + (t + 1) * 128]
                        E.do(lambda g=g, lhs=lhs: E.h.matmul(
                            pTf[:, 0:512], lhs,
                            wo_bf[:, g * 1024 + 512:(g + 1) * 1024],
                            start=(g == 0), stop=(g == 3)),
                            inc=("st" if g == 3 else None), n=1,
                            mark=(("womm", s, t, 1) if g == 3 else None))

                def tAT(tp):
                    slp = (tp % 2) * 1024
                    E.wait("ss", ("castback", s, tp))
                    E.wait("sv", ("aoev", s, tp + 1) if tp + 1 < NT else
                           ("ffnev", s, tp - 1))
                    for g in range(8):
                        E.do(lambda slp=slp, g=g: E.h.transpose(
                            pT[:, g * 128:(g + 1) * 128],
                            abf[:, slp + g * 128:slp + (g + 1) * 128],
                            ident[:, :]),
                            inc=("st" if g == 7 else None), n=1,
                            mark=(("atT", s, tp) if g == 7 else None))

                def tF0(tp):
                    E.wait("sv", ("atevac", s, tp))
                    E.wait("sv", ("aoev0", s, tp + 1) if tp + 1 < NT else
                           ("ffo0", s, tp - 1))
                    slp = (tp % 2) * 1024
                    for g in range(8):
                        lhs = atT[:, slp + g * 128:slp + (g + 1) * 128]
                        E.do(lambda g=g, lhs=lhs: E.h.matmul(
                            pW[:, 0:512], lhs,
                            wt_bf[:, g * 1024:g * 1024 + 512],
                            start=(g == 0), stop=(g == 7)),
                            inc=("st" if g == 7 else None), n=1,
                            mark=(("ffnmm", s, tp, 0) if g == 7 else None))

                def tF1(tp):
                    slp = (tp % 2) * 1024
                    for g in range(8):
                        lhs = atT[:, slp + g * 128:slp + (g + 1) * 128]
                        E.do(lambda g=g, lhs=lhs: E.h.matmul(
                            pTf[:, 0:512], lhs,
                            wt_bf[:, g * 1024 + 512:(g + 1) * 1024],
                            start=(g == 0), stop=(g == 7)),
                            inc=("st" if g == 7 else None), n=1,
                            mark=(("ffnmm", s, tp, 1) if g == 7 else None))

                def p4sched(u0):
                    u1 = u0 + 1
                    out = []

                    def add(slot, fn, u, front):
                        if front and u < NT:
                            out.append((slot, lambda fn=fn, u=u: fn(u)))
                        elif not front and 1 <= u <= NT:
                            out.append((slot, lambda fn=fn, u=u: fn(u - 1)))
                    add(0, tW0, u0, True)
                    add(1, tW1, u0, True)
                    add(2, tAT, u0, False)
                    add(3, tF0, u0, False)
                    add(4, tF1, u0, False)
                    add(9, tW0, u1, True)
                    add(10, tW1, u1, True)
                    add(11, tAT, u1, False)
                    add(12, tF0, u1, False)
                    add(13, tF1, u1, False)
                    return out

                i = 0
                step = 0
                for qb in range(NQB):
                    pieces = []
                    if qb >= 1:
                        pieces = p4sched(step)
                        step += 2
                    pi = 0
                    k = 0
                    while i < len(ENTRIES) and ENTRIES[i]["qb"] == qb:
                        emit_scores(i, ENTRIES[i])
                        if i >= 2:
                            emit_pv(i - 2, ENTRIES[i - 2])
                        i += 1
                        k += 1
                        while pi < len(pieces) and pieces[pi][0] <= k - 3:
                            pieces[pi][1]()
                            pi += 1
                    while pi < len(pieces):
                        pieces[pi][1]()
                        pi += 1
                emit_pv(len(ENTRIES) - 2, ENTRIES[-2])
                emit_pv(len(ENTRIES) - 1, ENTRIES[-1])
                for u0 in (NT - 2, NT):
                    for _, fn in p4sched(u0):
                        fn()
                del emit_scores, emit_pv, p4sched, tW0, tW1, tAT, tF0, tF1

        streams = dict(sync=stream_sync, gpsimd=stream_gpsimd,
                       vector=stream_vector, scalar=stream_scalar,
                       tensor=stream_tensor)
        for name, fn in streams.items():
            fn(Eng(sched, None, sems, dry=True))
        block.sync(lambda eng: streams["sync"](Eng(sched, eng, sems, dry=False)))
        block.gpsimd(lambda eng: streams["gpsimd"](Eng(sched, eng, sems, dry=False)))
        block.vector(lambda eng: streams["vector"](Eng(sched, eng, sems, dry=False)))
        block.scalar(lambda eng: streams["scalar"](Eng(sched, eng, sems, dry=False)))
        block.tensor(lambda eng: streams["tensor"](Eng(sched, eng, sems, dry=False)))

    return nc


_NC_CACHE = None


def kernel(L0_attn_x, L0_attn_Wq, L0_attn_Wkv, L0_attn_Wk, L0_attn_Wv,
           L0_attn_Wo, L0_attn_q_g, L0_attn_k_g, L0_ffn_weight,
           L0_ffn_weight_scale, L0_ffn_bias):
    global _NC_CACHE
    x = np.ascontiguousarray(np.asarray(L0_attn_x, dtype=np.float32))
    common = {
        "wq": np.ascontiguousarray(np.asarray(L0_attn_Wq, dtype=np.float32)),
        "wkv": np.ascontiguousarray(np.asarray(L0_attn_Wkv, dtype=np.float32)),
        "wk": np.ascontiguousarray(np.asarray(L0_attn_Wk, dtype=np.float32)),
        "wv": np.ascontiguousarray(np.asarray(L0_attn_Wv, dtype=np.float32)),
        "wo": np.ascontiguousarray(np.asarray(L0_attn_Wo, dtype=np.float32)),
        "qg": np.asarray(L0_attn_q_g, dtype=np.float32).reshape(1, HD),
        "kg": np.asarray(L0_attn_k_g, dtype=np.float32).reshape(1, ROPE),
        "w8": np.ascontiguousarray(np.asarray(L0_ffn_weight).view(np.uint8)),
        "ws": np.asarray(L0_ffn_weight_scale, dtype=np.float32).reshape(1, 1),
        "fb": np.asarray(L0_ffn_bias, dtype=np.float32).reshape(1, D),
    }
    if _NC_CACHE is None:
        _NC_CACHE = build_core()
    in_maps = []
    for c in range(NCORES):
        m = dict(common)
        m["x"] = x[c * SEQS:(c + 1) * SEQS]
        in_maps.append(m)
    res = run_bass_kernel_spmd(_NC_CACHE, in_maps, core_ids=list(range(NCORES)))
    return np.concatenate([r["out"] for r in res.results], axis=0)


# version 22
# speedup vs baseline: 1.0393x; 1.0393x over previous
import numpy as np
import ml_dtypes
from contextlib import ExitStack

import concourse.bass as bass
import concourse.mybir as mybir
from concourse.bass_utils import run_bass_kernel_spmd

F32 = mybir.dt.float32
BF16 = mybir.dt.bfloat16
FP8 = mybir.dt.float8e4
U8 = mybir.dt.uint8
AF = mybir.ActivationFunctionType
ALU = mybir.AluOpType
AX = mybir.AxisListType

NCORES = 8
SEQS = 2
T = 2048
D = 1024
H = 16
HD = 64
ROPE = 32
LAT = 128
NT = T // 128
QB = 256
NQB = T // QB
NG = 4
EPS_LN = 1e-5
EPS_RMS = 1e-6


def chunk_entries():
    out = []
    for qb in range(NQB):
        for g in range(NG):
            nch = 2 * qb + 2
            for c in range(nch):
                out.append(dict(qb=qb, g=g, c=c,
                                qoff=(128 if c == 2 * qb + 1 else 0),
                                first=(c == 0), last=(c == nch - 1),
                                diag=(c >= 2 * qb)))
    return out


ENTRIES = chunk_entries()
LAST_G, LAST_QB = ENTRIES[-1]["g"], ENTRIES[-1]["qb"]


class Sched:
    def __init__(self):
        self.counts = {}
        self.marks = {}

    def inc(self, sem, n):
        self.counts[sem] = self.counts.get(sem, 0) + n

    def mark(self, sem, key):
        self.marks[(sem, key)] = self.counts.get(sem, 0)

    def val(self, sem, key):
        return self.marks.get((sem, key), 0)


class Eng:
    def __init__(self, sched, handle, sems, dry):
        self.s = sched
        self.h = handle
        self.sems = sems
        self.dry = dry
        self.gates = []

    def push_gate(self, b):
        self.gates.append(b)

    def pop_gate(self):
        self.gates.pop()

    def do(self, fn, inc=None, n=1, mark=None):
        if self.gates and not all(self.gates):
            return
        if self.dry:
            if inc is not None:
                self.s.inc(inc, n)
                if mark is not None:
                    self.s.mark(inc, mark)
            return
        instr = fn()
        if inc is not None:
            instr.then_inc(self.sems[inc], n)

    def dobar(self, fn):
        self.svb_count = getattr(self, "svb_count", 0) + 1
        self.do(fn, inc="svb", n=1)

    def waitbar(self):
        if self.gates and not all(self.gates):
            return
        if self.dry:
            return
        if getattr(self, "svb_count", 0) > 0:
            self.h.wait_ge(self.sems["svb"], self.svb_count)

    def wait(self, sem, key):
        if self.gates and not all(self.gates):
            return
        if self.dry:
            return
        v = self.s.val(sem, key)
        if v > 0:
            self.h.wait_ge(self.sems[sem], v)


def build_core(phases=99):
    pass_seqs = SEQS if phases >= 4 else 1

    nc = bass.Bass("TRN2", target_bir_lowering=False,
                   detect_race_conditions=False)

    x_in = nc.dram_tensor("x", [SEQS, T, D], F32, kind="ExternalInput").ap()
    wq_in = nc.dram_tensor("wq", [D, D], F32, kind="ExternalInput").ap()
    wkv_in = nc.dram_tensor("wkv", [D, LAT], F32, kind="ExternalInput").ap()
    wk_in = nc.dram_tensor("wk", [LAT, 512], F32, kind="ExternalInput").ap()
    wv_in = nc.dram_tensor("wv", [LAT, 512], F32, kind="ExternalInput").ap()
    wo_in = nc.dram_tensor("wo", [D, D], F32, kind="ExternalInput").ap()
    qg_in = nc.dram_tensor("qg", [1, HD], F32, kind="ExternalInput").ap()
    kg_in = nc.dram_tensor("kg", [1, ROPE], F32, kind="ExternalInput").ap()
    w8_in = nc.dram_tensor("w8", [D, D], U8, kind="ExternalInput").ap()
    ws_in = nc.dram_tensor("ws", [1, 1], F32, kind="ExternalInput").ap()
    fb_in = nc.dram_tensor("fb", [1, D], F32, kind="ExternalInput").ap()
    out = nc.dram_tensor("out", [SEQS, T, D], F32, kind="ExternalOutput").ap()

    sched = Sched()
    do_p3 = phases >= 2
    do_p4 = phases >= 3
    nseq = SEQS if phases >= 4 else (1 if phases < 4 else SEQS)


    with ExitStack() as st:
        e = st.enter_context

        wq_bf = e(nc.sbuf_tensor("wq_bf", [128, 8 * 1024], BF16))
        wkv_bf = e(nc.sbuf_tensor("wkv_bf", [128, 8 * 128], BF16))
        wk_bf = e(nc.sbuf_tensor("wk_bf", [128, 512], BF16))
        wv_bf = e(nc.sbuf_tensor("wv_bf", [128, 512], BF16))
        wo_bf = e(nc.sbuf_tensor("wo_bf", [128, 4 * 1024], BF16))
        wt_bf = e(nc.sbuf_tensor("wt_bf", [128, 8 * 1024], BF16))
        bias_bc = e(nc.sbuf_tensor("bias_bc", [128, 1024], F32))
        qg8_bc = e(nc.sbuf_tensor("qg8_bc", [128, 512], BF16))
        kg_bc = e(nc.sbuf_tensor("kg_bc", [128, 512], BF16))
        ident = e(nc.sbuf_tensor("ident", [128, 128], BF16))
        tri = e(nc.sbuf_tensor("tri", [128, 128], BF16))
        ones32 = e(nc.sbuf_tensor("ones32", [128, 32], BF16))
        onesrow = e(nc.sbuf_tensor("onesrow", [1, 128], F32))
        ubws = e(nc.sbuf_tensor("ubws", [128, 2], F32))
        ubias = ubws[:, 0:1]
        ws128 = ubws[:, 1:2]
        qg_sb = e(nc.sbuf_tensor("qg_sb", [1, HD], F32))
        kg_sb = e(nc.sbuf_tensor("kg_sb", [1, ROPE], F32))
        ws_sb = e(nc.sbuf_tensor("ws_sb", [1, 1], F32))
        fb_sb = e(nc.sbuf_tensor("fb_sb", [1, D], F32))
        sm_sb = e(nc.sbuf_tensor("sm_sb", [1, 8], F32))
        eps_sb = e(nc.sbuf_tensor("eps_sb", [128, 1], F32))
        zrow = e(nc.sbuf_tensor("zrow", [128, 256], BF16))


        qhT = e(nc.sbuf_tensor("qhT", [128, NG * T], BF16))
        khT = e(nc.sbuf_tensor("khT", [128, NG * T], BF16))
        vtok = e(nc.sbuf_tensor("vtok", [128, NT * 512], BF16))
        otf = e(nc.sbuf_tensor("otf", [128, NG * T], BF16))
        pt_ring = e(nc.sbuf_tensor("pt_ring", [128, 4 * 1024], BF16))

        xf = e(nc.sbuf_tensor("xf", [128, 2 * 1024], F32))
        xb = e(nc.sbuf_tensor("xb", [128, 2 * 1024], F32))
        ln1 = e(nc.sbuf_tensor("ln1", [128, 2 * 1024], BF16))
        lnT = e(nc.sbuf_tensor("lnT", [128, 2 * 1024], BF16))
        qf = e(nc.sbuf_tensor("qf", [128, 1024], F32))
        qsq = e(nc.sbuf_tensor("qsq", [128, 1024], F32))
        qhat = e(nc.sbuf_tensor("qhat", [128, 512], BF16))
        khat = e(nc.sbuf_tensor("khat", [128, 512], BF16))
        kvsb = e(nc.sbuf_tensor("kvsb", [128, 128], BF16))
        kvT = e(nc.sbuf_tensor("kvT", [128, 128], BF16))
        rtmp = e(nc.sbuf_tensor("rtmp", [128, 512], F32))
        stats = e(nc.sbuf_tensor("stats", [128, 128], F32))
        rq = e(nc.sbuf_tensor("rq", [128, 32], F32))
        sums_g = e(nc.sbuf_tensor("sums_g", [128, 256], F32))
        sums_r = e(nc.sbuf_tensor("sums_r", [128, 256], F32))
        aosb = e(nc.sbuf_tensor("aosb", [128, 1024], F32))
        xmid = e(nc.sbuf_tensor("xmid", [128, 2 * 1024], F32))
        xadj = e(nc.sbuf_tensor("xadj", [128, 1024], F32))
        qin8 = e(nc.sbuf_tensor("qin8", [128, 1024], FP8))
        abf = e(nc.sbuf_tensor("abf", [128, 2 * 1024], BF16))
        atT = e(nc.sbuf_tensor("atT", [128, 2 * 1024], BF16))
        ffo = e(nc.sbuf_tensor("ffo", [128, 1024], F32))
        xout = e(nc.sbuf_tensor("xout", [128, 2 * 1024], F32))

        pS = e(nc.psum_tensor("pS", [128, 2048], F32))
        pA = pS[:, 0:1024]
        pB = pS[:, 1024:2048]
        pC = e(nc.psum_tensor("pC", [128, 512], F32))
        pT = e(nc.psum_tensor("pT", [128, 1024], BF16))
        pW = e(nc.psum_tensor("pW", [128, 512], F32))
        pT2 = pW.bitcast(BF16)
        pTf = pT.bitcast(F32)
        pD = e(nc.psum_tensor("pD", [128, 512], F32))
        pDb = pD.bitcast(BF16)

        xmid_u8 = xmid[:, :].bitcast(U8)
        dec_u = xmid_u8[:, 0:1024]
        dec_a = xmid_u8[:, 1024:2048]
        dec_b = xmid_u8[:, 2048:3072]
        dec_sg = xmid_u8[:, 3072:4096]
        dec_m = xmid_u8[:, 4096:5120]
        dec_A = abf[:, 0:1024]
        dec_B = abf[:, 1024:2048]
        wn_bf = atT
        w8_st = xout[:, :].bitcast(U8)
        rowbuf = qsq

        sems = {n: e(nc.semaphore(n)) for n in
                ["svb", "stb", "st", "sv", "ss", "sg", "ssm", "swq0", "swq1", "swo0",
                 "swo1", "skv", "swk", "swv", "sw80", "sw81", "sx0", "sx1",
                 "sb0", "sb1", "so0", "so1"]}
        block = e(nc.Block())


        def stream_sync(E):
            E.do(lambda: E.h.dma_start(qg_sb[:, :], qg_in), inc="ssm", n=16)
            E.do(lambda: E.h.dma_start(kg_sb[:, :], kg_in), inc="ssm", n=16)
            E.do(lambda: E.h.dma_start(ws_sb[:, :], ws_in), inc="ssm", n=16)
            E.do(lambda: E.h.dma_start(fb_sb[:, :], fb_in), inc="ssm", n=16,
                 mark="smalls")
            for g in range(8):
                E.wait("sv", ("wqcast", g - 2))
                E.do(lambda g=g: E.h.dma_start(
                    xf[:, (g % 2) * 1024:(g % 2 + 1) * 1024],
                    wq_in[g * 128:(g + 1) * 128, :]), inc="swq%d" % (g % 2),
                    n=16, mark=("wq", g))
            for g in range(4):
                E.wait("sv", ("wocast", g - 2))
                src = bass.AP(wo_in.tensor, (4 * g) * 64 * 1024,
                              [[64 * 1024, 4], [1024, 32], [1, 1024]])
                E.do(lambda g=g, src=src: E.h.dma_start(
                    xb[:, (g % 2) * 1024:(g % 2 + 1) * 1024], src),
                    inc="swo%d" % (g % 2), n=16, mark=("wo", g))
            E.wait("sv", ("wqcast", 7))
            E.do(lambda: E.h.dma_start(
                xf[:, 0:1024].rearrange("p (a b) -> p a b", a=8),
                wkv_in.rearrange("(c p) l -> p c l", p=128)), inc="skv", n=16,
                mark="wkv")
            E.wait("sv", ("wocast", 3))
            E.do(lambda: E.h.dma_start(xb[:, 0:512], wk_in), inc="swk", n=16,
                 mark="wk")
            E.do(lambda: E.h.dma_start(xb[:, 512:1024], wv_in), inc="swv", n=16,
                 mark="wv")
            for j in range(8):
                E.wait("sv", ("decsel", j - 2))
                E.do(lambda j=j: E.h.dma_start(
                    w8_st[:, (j % 2) * 1024:(j % 2 + 1) * 1024],
                    w8_in[j * 128:(j + 1) * 128, :]), inc="sw8%d" % (j % 2),
                    n=16, mark=("w8", j))
            for s in range(SEQS):
                for t in range(NT):
                    if s == 0 and t < 2:
                        E.wait("sv", "wkvcast")
                    else:
                        E.wait("ss", ("ln1", s, t - 2) if t >= 2 else
                               ("ln1", s - 1, NT + t - 2))
                    E.do(lambda s=s, t=t: E.h.dma_start(
                        xf[:, (t % 2) * 1024:(t % 2 + 1) * 1024],
                        x_in[s, t * 128:(t + 1) * 128, :]),
                        inc="sx%d" % (t % 2), n=16, mark=("x", s, t))
                for t in range(NT):
                    if s == 0 and t < 2:
                        E.wait("sv", "wkwvcast")
                    else:
                        E.wait("sv", ("ln2res", s, t - 2) if t >= 2 else
                               ("ln2res", s - 1, NT + t - 2))
                    E.do(lambda s=s, t=t: E.h.dma_start(
                        xb[:, (t % 2) * 1024:(t % 2 + 1) * 1024],
                        x_in[s, t * 128:(t + 1) * 128, :]),
                        inc="sb%d" % (t % 2), n=16, mark=("xb", s, t))

        def stream_gpsimd(E):
            E.do(lambda: E.h.memset(ident[:, :], 0.0), inc="sg", n=1,
                 mark="c_mset")
            E.do(lambda: E.h.memset(tri[:, :], 1.0))
            E.do(lambda: E.h.memset(ones32[:, :], 1.0))
            E.do(lambda: E.h.memset(onesrow[:, :], 1.0))
            E.do(lambda: E.h.memset(eps_sb[:, :], EPS_LN))
            E.do(lambda: E.h.memset(zrow[:, :], 0.0), inc="sg", n=1,
                 mark="c_mset2")
            E.wait("sg", "c_mset2")
            E.do(lambda: E.h.affine_select(
                out=ident[:, :], in_=ident[:, :], compare_op=ALU.not_equal,
                fill=1.0, base=0, pattern=[[-1, 128]], channel_multiplier=1))
            E.do(lambda: E.h.affine_select(
                out=tri[:, :], in_=tri[:, :], compare_op=ALU.is_ge,
                fill=0.0, base=0, pattern=[[1, 128]], channel_multiplier=-1),
                inc="sg", n=1, mark="consts")
            for s in range(SEQS):
                for t in range(NT):
                    E.wait("sv", ("xout", s, t))
                    E.do(lambda s=s, t=t: E.h.dma_start(
                        out[s, t * 128:(t + 1) * 128, :],
                        xout[:, (t % 2) * 1024:(t % 2 + 1) * 1024]),
                        inc="so%d" % (t % 2), n=16, mark=("out", s, t))
            E.wait("so0", ("out", SEQS - 1, NT - 2))
            E.wait("so1", ("out", SEQS - 1, NT - 1))

        def combine_ln(E, src, dst, inv_n, inc=None, mark=None):
            alu = ALU
            E.do(lambda: E.h.bn_aggr(stats[:, 12:14], stats[:, src:src + 6]))
            E.dobar(lambda: E.h.bn_aggr(stats[:, 14:16], stats[:, src + 6:src + 12]))
            E.waitbar()
            E.do(lambda: E.h.tensor_scalar(
                stats[:, dst:dst + 1], stats[:, 12:13], stats[:, 14:15], 0.5,
                alu.add, alu.mult))
            E.dobar(lambda: E.h.tensor_scalar(
                stats[:, 124:125], stats[:, 12:13], stats[:, 14:15], None,
                alu.subtract))
            E.waitbar()
            E.do(lambda: E.h.tensor_scalar(
                stats[:, 125:126], stats[:, 124:125], stats[:, 124:125], 0.25,
                alu.mult, alu.mult))
            E.dobar(lambda: E.h.tensor_scalar(
                stats[:, dst + 1:dst + 2], stats[:, 13:14], stats[:, 15:16], 0.5,
                alu.add, alu.mult))
            E.waitbar()
            E.do(lambda: E.h.tensor_scalar(
                stats[:, dst + 1:dst + 2], stats[:, dst + 1:dst + 2],
                stats[:, 125:126], None, alu.add), inc=inc, n=1, mark=mark)

        def stream_vector(E):
            alu = ALU
            for g in range(8):
                E.wait("swq%d" % (g % 2), ("wq", g))
                E.do(lambda g=g: E.h.tensor_copy(
                    wq_bf[:, g * 1024:(g + 1) * 1024],
                    xf[:, (g % 2) * 1024:(g % 2 + 1) * 1024]),
                    inc="sv", n=1, mark=("wqcast", g))
            for g in range(4):
                E.wait("swo%d" % (g % 2), ("wo", g))
                E.do(lambda g=g: E.h.tensor_copy(
                    wo_bf[:, g * 1024:(g + 1) * 1024],
                    xb[:, (g % 2) * 1024:(g % 2 + 1) * 1024]),
                    inc="sv", n=1, mark=("wocast", g))
            E.wait("skv", "wkv")
            E.do(lambda: E.h.tensor_copy(wkv_bf[:, :], xf[:, 0:1024]),
                 inc="sv", n=1, mark="wkvcast")
            E.wait("swk", "wk")
            E.do(lambda: E.h.tensor_copy(wk_bf[:, :], xb[:, 0:512]))
            E.wait("swv", "wv")
            E.do(lambda: E.h.tensor_copy(wv_bf[:, :], xb[:, 512:1024]),
                 inc="sv", n=1, mark="wkwvcast")
            E.wait("ssm", "smalls")
            E.do(lambda: E.h.tensor_scalar(
                rowbuf[0:1, 0:512].rearrange("p (h r) -> p h r", h=16),
                qg_sb[0:1, 0:ROPE].unsqueeze(1).broadcast_to([1, 16, ROPE]),
                0.125, None, alu.mult))
            E.dobar(lambda: E.h.tensor_copy(
                rowbuf[0:1, 512:1024].rearrange("p (h r) -> p h r", h=16),
                kg_sb[0:1, 0:ROPE].unsqueeze(1).broadcast_to([1, 16, ROPE])))
            E.waitbar()
            E.do(lambda: E.h.tensor_reduce(
                sm_sb[0:1, 0:1].unsqueeze(2), rowbuf[0:1, 0:32].unsqueeze(1),
                AX.X, alu.max, apply_absolute_value=True))
            E.dobar(lambda: E.h.tensor_reduce(
                sm_sb[0:1, 1:2].unsqueeze(2), kg_sb[0:1, 0:ROPE].unsqueeze(1),
                AX.X, alu.max, apply_absolute_value=True))
            E.waitbar()
            E.do(lambda: E.h.tensor_scalar(
                sm_sb[0:1, 3:4], sm_sb[0:1, 0:1], sm_sb[0:1, 1:2],
                -22.627416997969522, alu.mult, alu.mult),
                inc="sv", n=1, mark="usmall")
            E.wait("st", "rank1a")
            E.do(lambda: E.h.tensor_copy(ubws[:, :], pC[:, 0:2]),
                 inc="sv", n=1, mark="evac_a")
            E.wait("st", "rank1b")
            E.do(lambda: E.h.tensor_copy(qg8_bc[:, :], pC[:, 0:512]),
                 inc="sv", n=1, mark="evac_b")
            E.wait("st", "rank1c")
            E.do(lambda: E.h.tensor_copy(kg_bc[:, :], pC[:, 0:512]))
            E.do(lambda: E.h.tensor_tensor(kg_bc[:, :], kg_bc[:, :],
                                           qg8_bc[:, :], alu.mult),
                 inc="sv", n=1, mark="evac_c")
            E.wait("st", "rank1d")
            E.do(lambda: E.h.tensor_copy(bias_bc[:, :], pA[:, :]),
                 inc="sv", n=1, mark="evac_d")
            for j in range(8):
                E.wait("sw8%d" % (j % 2), ("w8", j))
                w8c = w8_st[:, (j % 2) * 1024:(j % 2 + 1) * 1024]
                E.do(lambda w8c=w8c: E.h.tensor_scalar(
                    dec_u, w8c, 0x7F, None, alu.bitwise_and))
                E.do(lambda w8c=w8c: E.h.tensor_scalar(
                    dec_sg, w8c, 0x80, None, alu.bitwise_and))
                E.do(lambda: E.h.tensor_scalar(
                    dec_a, dec_u, 0x77, None, alu.min))
                E.do(lambda: E.h.tensor_scalar(
                    dec_b, dec_u, 16, 8, alu.max, alu.subtract))
                E.do(lambda: E.h.tensor_scalar(
                    dec_m, dec_u, 0x78, None, alu.is_ge))
                E.do(lambda: E.h.tensor_tensor(
                    dec_a, dec_a, dec_sg, alu.bitwise_or))
                E.do(lambda: E.h.tensor_tensor(
                    dec_b, dec_b, dec_sg, alu.bitwise_or),
                    inc="sv", n=1, mark=("decbits", j))
                E.wait("ss", ("deccast", j))
                if j >= 2:
                    E.wait("st", ("wtT", j - 2))
                E.do(lambda j=j: E.h.select(
                    wn_bf[:, (j % 2) * 1024:(j % 2 + 1) * 1024],
                    dec_m, dec_B, dec_A),
                    inc="sv", n=1, mark=("decsel", j))
                E.wait("st", ("wtT", j))
                E.do(lambda j=j: E.h.tensor_copy(
                    wt_bf[:, :].rearrange("p (g n) -> p g n", g=8)[:, :, j * 128:(j + 1) * 128],
                    pT[:, :].rearrange("p (g n) -> p g n", g=8)),
                    inc="sv", n=1, mark=("wtevac", j))

            for s in range(SEQS):

                def p1A(t):
                    sl = (t % 2) * 1024
                    d = 16 + 40 * (t % 2)
                    xt = xf[:, sl:sl + 1024]
                    E.wait("sx%d" % (t % 2), ("x", s, t))
                    E.do(lambda xt=xt: E.h.bn_stats(stats[:, 0:6], xt[:, 0:512]))
                    E.do(lambda xt=xt: E.h.bn_stats(stats[:, 6:12], xt[:, 512:1024]))
                    combine_ln(E, 0, d, 1.0 / 1024)
                    E.do(lambda d=d: E.h.tensor_scalar(
                        stats[:, d + 3:d + 4], stats[:, d:d + 1], -1.0, None,
                        alu.mult), inc="sv", n=1, mark=("ln1stats", s, t))

                def p1E(t):
                    E.wait("ss", ("qkvfree", s, t))
                    E.do(lambda: E.h.tensor_tensor(qsq[:, :], qf[:, :], qf[:, :],
                                                   alu.mult))
                    E.do(lambda: E.h.tensor_reduce(
                        stats[:, 20:36].unsqueeze(2),
                        qsq[:, :].rearrange("p (h d) -> p h d", h=16), AX.X,
                        alu.add))
                    E.do(lambda: E.h.tensor_tensor(
                        qsq[:, 0:512], xadj[:, 0:512], xadj[:, 0:512], alu.mult))
                    E.dobar(lambda: E.h.tensor_reduce(
                        stats[:, 36:52].unsqueeze(2),
                        qsq[:, 0:512].rearrange("p (h d) -> p h d", h=16),
                        AX.X, alu.add))
                    E.waitbar()
                    E.do(lambda: E.h.tensor_scalar(
                        stats[:, 20:36], stats[:, 20:36], 1.0 / 64, EPS_RMS,
                        alu.mult, alu.add))
                    E.do(lambda: E.h.tensor_scalar(
                        stats[:, 36:52], stats[:, 36:52], 1.0 / 32, EPS_RMS,
                        alu.mult, alu.add),
                        inc="sv", n=1, mark=("rmsstage", s, t))
                    E.wait("ss", ("rmsr", s, t))
                    E.do(lambda: E.h.tensor_tensor(
                        qhat[:, :].rearrange("p (h r) -> p h r", h=16),
                        qf[:, :].rearrange("p (h d) -> p h d", h=16)[:, :, 0:ROPE],
                        rq[:, 0:16].unsqueeze(2).broadcast_to([128, 16, ROPE]),
                        alu.mult),
                        inc="sv", n=1, mark=("qpost", s, t))
                    E.do(lambda: E.h.tensor_tensor(
                        rtmp[:, :].rearrange("p (h r) -> p h r", h=16),
                        xadj[:, 0:512].rearrange("p (h d) -> p h d", h=16),
                        rq[:, 16:32].unsqueeze(2).broadcast_to([128, 16, ROPE]),
                        alu.mult))
                    E.do(lambda: E.h.tensor_tensor(
                        khat[:, :], rtmp[:, :], kg_bc[:, :], alu.mult),
                        inc="sv", n=1, mark=("kpost", s, t))

                def p1Q(t):
                    E.wait("st", ("qkT", s, t))
                    E.do(lambda t=t: E.h.tensor_copy(
                        qhT[:, :].rearrange("p (g n) -> p g n", g=NG)[:, :, t * 128:(t + 1) * 128],
                        pDb[:, 0:512].rearrange("p (g n) -> p g n", g=NG)))
                    E.do(lambda t=t: E.h.tensor_copy(
                        khT[:, :].rearrange("p (g n) -> p g n", g=NG)[:, :, t * 128:(t + 1) * 128],
                        pDb[:, 512:1024].rearrange("p (g n) -> p g n", g=NG)),
                        inc="sv", n=1, mark=("qkTevac", s, t))

                for u in range(NT + 2):
                    if u == 0:
                        p1A(0)
                    if u + 1 < NT:
                        p1A(u + 1)
                    if 2 <= u <= NT + 1:
                        p1E(u - 2)
                        p1Q(u - 2)
                del p1A, p1E, p1Q
                E.do(lambda: E.h.tensor_copy(pC[:, 0:256], zrow[:, :]))
                E.do(lambda: E.h.tensor_copy(pD[:, 0:256], zrow[:, :]),
                     inc="sv", n=1, mark=("pvz", s))

                def p3e(i, en):
                    slot = i % 4
                    if en["diag"]:
                        E.wait("ss", ("exp", s, i))
                        coff = en["qoff"]
                        pv = pt_ring[:, slot * 1024:(slot + 1) * 1024]
                        E.do(lambda pv=pv, coff=coff: E.h.tensor_tensor(
                            pv.rearrange("p (h q) -> p h q", h=4)[:, :, coff:coff + 128],
                            pv.rearrange("p (h q) -> p h q", h=4)[:, :, coff:coff + 128],
                            tri[:, :].unsqueeze(1).broadcast_to([128, 4, 128]),
                            alu.mult), inc="sv", n=1, mark=("mask", s, i))

                def p3l(i, en):
                    if en["last"]:
                        E.wait("st", ("pv", s, i))
                        g, qb = en["g"], en["qb"]
                        E.dobar(lambda: E.h.tensor_scalar(
                            sums_g[:, :], pD[:, 0:256], 1e-35, None, alu.max))
                        E.waitbar()
                        E.dobar(lambda: E.h.reciprocal(sums_r[:, :],
                                                       sums_g[:, :]))
                        E.waitbar()
                        E.do(lambda g=g, qb=qb: E.h.tensor_tensor(
                            otf[:, g * T + qb * QB: g * T + (qb + 1) * QB],
                            pC[:, 0:256], sums_r[:, :], alu.mult))
                        E.do(lambda: E.h.tensor_copy(pC[:, 0:256], zrow[:, :]))
                        E.do(lambda g=g, qb=qb: E.h.tensor_copy(
                            pD[:, 0:256], zrow[:, :]),
                            inc="sv", n=1, mark=("ot", s, g, qb))


                def vA0(t):
                    E.wait("st", ("womm", s, t, 0))
                    E.do(lambda: E.h.tensor_copy(aosb[:, 0:512], pW[:, :]),
                         inc="sv", n=1, mark=("aoev0", s, t))
                    E.do(lambda: E.h.bn_stats(stats[:, 0:6], aosb[:, 0:512]))

                def vA1(t):
                    d = 60 + 24 * (t % 2)
                    E.wait("st", ("womm", s, t, 1))
                    E.do(lambda: E.h.tensor_copy(aosb[:, 512:1024], pTf[:, :]),
                         inc="sv", n=1, mark=("aoev", s, t))
                    E.do(lambda: E.h.bn_stats(stats[:, 6:12], aosb[:, 512:1024]))
                    combine_ln(E, 0, d, 1.0 / 1024, inc="sv",
                               mark=("ln2stats", s, t))

                def vD(tp):
                    slp = (tp % 2) * 1024
                    E.wait("st", ("atT", s, tp))
                    E.do(lambda slp=slp: E.h.tensor_copy(
                        atT[:, slp:slp + 1024], pT[:, :]),
                        inc="sv", n=1, mark=("atevac", s, tp))

                def vE0(tp):
                    dp = 60 + 24 * (tp % 2)
                    E.wait("st", ("ffnmm", s, tp, 0))
                    E.do(lambda dp=dp: E.h.scalar_tensor_tensor(
                        ffo[:, 0:512], pW[:, :], stats[:, dp + 12:dp + 13],
                        bias_bc[:, 0:512], alu.mult, alu.add),
                        inc="sv", n=1, mark=("ffo0", s, tp))
                    E.do(lambda: E.h.bn_stats(stats[:, 0:6], ffo[:, 0:512]))

                def vE1(tp):
                    dp = 60 + 24 * (tp % 2)
                    E.wait("st", ("ffnmm", s, tp, 1))
                    E.do(lambda dp=dp: E.h.scalar_tensor_tensor(
                        ffo[:, 512:1024], pTf[:, :], stats[:, dp + 12:dp + 13],
                        bias_bc[:, 512:1024], alu.mult, alu.add),
                        inc="sv", n=1, mark=("ffnev", s, tp))
                    E.do(lambda: E.h.bn_stats(stats[:, 6:12], ffo[:, 512:1024]))
                    combine_ln(E, 0, dp + 16, 1.0 / 1024, inc="sv",
                               mark=("ln4stats", s, tp))

                def vB(t):
                    d = 60 + 24 * (t % 2)
                    sl = (t % 2) * 1024
                    xbt = xb[:, sl:sl + 1024]
                    xm = xmid[:, sl:sl + 1024]
                    E.wait("ss", ("rstd2", s, t))
                    E.wait("sb%d" % (t % 2), ("xb", s, t))
                    E.dobar(lambda d=d: E.h.tensor_scalar(
                        stats[:, d + 3:d + 4], stats[:, d:d + 1],
                        stats[:, d + 2:d + 3], -1.0, alu.mult, alu.mult))
                    E.waitbar()
                    E.do(lambda xbt=xbt, d=d: E.h.tensor_scalar(
                        xadj[:, :], xbt, stats[:, d + 3:d + 4], None,
                        alu.add))
                    E.do(lambda sl=sl, d=d: E.h.scalar_tensor_tensor(
                        xmid[:, sl:sl + 1024], aosb[:, :],
                        stats[:, d + 2:d + 3], xadj[:, :], alu.mult,
                        alu.add), inc="sv", n=1, mark=("ln2res", s, t))
                    E.do(lambda xm=xm: E.h.bn_stats(stats[:, 0:6],
                                                    xm[:, 0:512]))
                    E.do(lambda xm=xm: E.h.bn_stats(stats[:, 6:12],
                                                    xm[:, 512:1024]))
                    combine_ln(E, 0, d + 4, 1.0 / 1024)
                    E.do(lambda xm=xm, d=d: E.h.tensor_reduce(
                        stats[:, d + 6:d + 7].unsqueeze(2), xm.unsqueeze(1),
                        AX.X, alu.max))
                    E.do(lambda xm=xm, d=d: E.h.tensor_reduce(
                        stats[:, d + 7:d + 8].unsqueeze(2), xm.unsqueeze(1),
                        AX.X, alu.min), inc="sv", n=1,
                        mark=("ln3stats", s, t))

                def vC(t):
                    d = 60 + 24 * (t % 2)
                    xm = xmid[:, (t % 2) * 1024:(t % 2 + 1) * 1024]
                    E.wait("ss", ("rstd3", s, t))
                    E.do(lambda d=d: E.h.tensor_scalar(
                        stats[:, d + 9:d + 10], stats[:, d + 6:d + 7],
                        stats[:, d + 4:d + 5], None, alu.subtract))
                    E.dobar(lambda d=d: E.h.tensor_scalar(
                        stats[:, d + 10:d + 11], stats[:, d + 4:d + 5],
                        stats[:, d + 7:d + 8], None, alu.subtract))
                    E.waitbar()
                    E.dobar(lambda d=d: E.h.tensor_scalar(
                        stats[:, d + 9:d + 10], stats[:, d + 9:d + 10],
                        stats[:, d + 10:d + 11], None, alu.max))
                    E.waitbar()
                    E.dobar(lambda d=d: E.h.tensor_scalar(
                        stats[:, d + 9:d + 10], stats[:, d + 9:d + 10],
                        stats[:, d + 8:d + 9], 1e-12, alu.mult, alu.max))
                    E.waitbar()
                    E.dobar(lambda d=d: E.h.tensor_scalar(
                        stats[:, d + 11:d + 12], stats[:, d + 9:d + 10],
                        1.0 / 224.0, None, alu.mult))
                    E.waitbar()
                    E.do(lambda d=d: E.h.tensor_scalar(
                        stats[:, d + 10:d + 11], stats[:, d + 11:d + 12],
                        0.0, None, alu.add))
                    E.dobar(lambda d=d: E.h.tensor_scalar(
                        stats[:, d + 12:d + 13], stats[:, d + 11:d + 12],
                        ws128, None, alu.mult))
                    E.waitbar()
                    E.dobar(lambda d=d: E.h.reciprocal(
                        stats[:, d + 13:d + 15], stats[:, d + 10:d + 12]))
                    E.waitbar()
                    E.dobar(lambda d=d: E.h.tensor_scalar(
                        stats[:, d + 13:d + 14], stats[:, d + 8:d + 9],
                        stats[:, d + 14:d + 15], None, alu.mult))
                    E.waitbar()
                    E.dobar(lambda d=d: E.h.tensor_scalar(
                        stats[:, d + 15:d + 16], stats[:, d + 4:d + 5],
                        stats[:, d + 13:d + 14], -1.0, alu.mult, alu.mult))
                    E.waitbar()
                    if t >= 1:
                        E.wait("ss", ("castback", s, t - 1))
                    elif s > 0:
                        E.wait("ss", ("castback", s - 1, NT - 1))
                    E.do(lambda xm=xm, d=d: E.h.tensor_scalar(
                        qin8[:, :], xm, stats[:, d + 13:d + 14],
                        stats[:, d + 15:d + 16], alu.mult, alu.add),
                        inc="sv", n=1, mark=("quant", s, t))

                def vF(tp):
                    dp = 60 + 24 * (tp % 2)
                    slp = (tp % 2) * 1024
                    xmp = xmid[:, slp:slp + 1024]
                    E.wait("ss", ("rstd4", s, tp))
                    E.dobar(lambda dp=dp: E.h.tensor_scalar(
                        stats[:, dp + 19:dp + 20], stats[:, dp + 16:dp + 17],
                        stats[:, dp + 18:dp + 19], -1.0, alu.mult, alu.mult))
                    E.waitbar()
                    E.do(lambda xmp=xmp, dp=dp: E.h.tensor_scalar(
                        xadj[:, :], xmp, stats[:, dp + 19:dp + 20], None,
                        alu.add))
                    if tp >= 2:
                        E.wait("so%d" % (tp % 2), ("out", s, tp - 2))
                    elif s > 0:
                        E.wait("so%d" % (tp % 2), ("out", s - 1, NT + tp - 2))
                    E.do(lambda slp=slp, dp=dp: E.h.scalar_tensor_tensor(
                        xout[:, slp:slp + 1024], ffo[:, :],
                        stats[:, dp + 18:dp + 19], xadj[:, :], alu.mult,
                        alu.add), inc="sv", n=1, mark=("xout", s, tp))

                def p4sched(u0):
                    u1 = u0 + 1
                    out = []

                    def add(slot, fn, u, front):
                        if front and u < NT:
                            out.append((slot, lambda fn=fn, u=u: fn(u)))
                        elif not front and 1 <= u <= NT:
                            out.append((slot, lambda fn=fn, u=u: fn(u - 1)))
                    add(0, vA0, u0, True)
                    add(1, vA1, u0, True)
                    add(2, vD, u0, False)
                    add(3, vE0, u0, False)
                    add(4, vE1, u0, False)
                    add(5, vB, u0, True)
                    add(6, vF, u0, False)
                    add(7, vC, u0, True)
                    add(9, vA0, u1, True)
                    add(10, vA1, u1, True)
                    add(11, vD, u1, False)
                    add(12, vE0, u1, False)
                    add(13, vE1, u1, False)
                    add(14, vB, u1, True)
                    add(15, vF, u1, False)
                    add(16, vC, u1, True)
                    return out

                i = 0
                step = 0
                for qb in range(NQB):
                    pieces = []
                    if qb >= 1:
                        pieces = p4sched(step)
                        step += 2
                    pi = 0
                    k = 0
                    while i < len(ENTRIES) and ENTRIES[i]["qb"] == qb:
                        p3e(i, ENTRIES[i])
                        if i >= 2:
                            p3l(i - 2, ENTRIES[i - 2])
                        i += 1
                        k += 1
                        while pi < len(pieces) and pieces[pi][0] <= k - 2:
                            pieces[pi][1]()
                            pi += 1
                    while pi < len(pieces):
                        pieces[pi][1]()
                        pi += 1
                p3l(len(ENTRIES) - 2, ENTRIES[-2])
                p3l(len(ENTRIES) - 1, ENTRIES[-1])
                for u0 in (NT - 2, NT):
                    for _, fn in p4sched(u0):
                        fn()
                del p3e, p3l, p4sched, vA0, vA1, vB, vC, vD, vE0, vE1, vF

        def stream_scalar(E):
            for j in range(8):
                E.wait("sv", ("decbits", j))
                if j >= 1:
                    E.wait("sv", ("decsel", j - 1))
                E.do(lambda: E.h.activation(dec_A, dec_a.bitcast(FP8),
                                            AF.Copy))
                E.do(lambda: E.h.activation(dec_B, dec_b.bitcast(FP8),
                                            AF.Copy, scale=2.0),
                     inc="ss", n=1, mark=("deccast", j))
            E.wait("sv", "evac_a")
            E.wait("sg", "consts")
            for s in range(SEQS):
                for t in range(NT + 2):
                    if t < NT:
                        d = 16 + 40 * (t % 2)
                        sl = (t % 2) * 1024
                        E.wait("sv", ("ln1stats", s, t))
                        E.do(lambda d=d: E.h.activation(
                            stats[:, d + 2:d + 3], stats[:, d + 1:d + 2],
                            AF.Ln, bias=eps_sb[:, 0:1]))
                        E.do(lambda d=d: E.h.activation(
                            stats[:, d + 2:d + 3], stats[:, d + 2:d + 3],
                            AF.Exp, scale=-0.5),
                             inc="ss", n=1, mark=("rstd1", s, t))
                        E.do(lambda d=d: E.h.activation(
                            stats[:, d + 3:d + 4], stats[:, d + 3:d + 4],
                            AF.Identity, scale=stats[:, d + 2:d + 3]))
                        E.do(lambda d=d, sl=sl: E.h.activation(
                            ln1[:, sl:sl + 1024], xf[:, sl:sl + 1024],
                            AF.Identity, bias=stats[:, d + 3:d + 4],
                            scale=stats[:, d + 2:d + 3]),
                            inc="ss", n=1, mark=("ln1", s, t))
                    if 1 <= t <= NT:
                        tp = t - 1
                        slp = (tp % 2) * 1024
                        trT = pT if tp % 2 == 0 else pT2
                        E.wait("st", ("lnTt", s, tp))
                        E.do(lambda slp=slp, trT=trT: E.h.activation(
                            lnT[:, slp:slp + 1024], trT[:, :], AF.Copy),
                            inc="ss", n=1, mark=("lnTevac", s, tp))
                        E.wait("st", ("kvmm", s, tp))
                        E.do(lambda: E.h.activation(kvT[:, :], pC[:, 256:384],
                                                    AF.Copy),
                             inc="ss", n=1, mark=("kvTevac", s, tp))
                        E.wait("st", ("kvvmm", s, tp))
                    if t >= 2:
                        E.wait("sv", ("rmsstage", s, t - 2))
                        E.do(lambda: E.h.activation(rq[:, :], stats[:, 20:52],
                                                    AF.Ln))
                        E.do(lambda: E.h.activation(rq[:, :], rq[:, :], AF.Exp,
                                                    scale=-0.5),
                             inc="ss", n=1, mark=("rmsr", s, t - 2))
                    if 1 <= t <= NT:
                        if tp >= 1:
                            E.wait("sv", ("kpost", s, tp - 1))
                        elif s > 0:
                            E.wait("sv", ("xout", s - 1, NT - 1))
                        E.do(lambda: E.h.activation(qf[:, :], pA[:, :], AF.Copy))
                        E.do(lambda: E.h.activation(xadj[:, 0:512], pB[:, 0:512],
                                                    AF.Copy))
                        E.do(lambda tp=tp: E.h.activation(
                            vtok[:, tp * 512:(tp + 1) * 512], pB[:, 512:1024],
                            AF.Copy),
                            inc="ss", n=1, mark=("qkvfree", s, tp))
                def p3x(i, en):
                    slot = i % 4
                    qoff = en["qoff"]
                    st_ = (i % 2) * 1024
                    E.wait("st", ("sc", s, i))
                    E.wait("st", ("pv", s, i - 4))
                    pv = pt_ring[:, slot * 1024:(slot + 1) * 1024]
                    src = pS[:, st_:st_ + 1024].rearrange(
                        "p (h q) -> p h q", h=4)[:, :, qoff:256]
                    dst = pv.rearrange("p (h q) -> p h q", h=4)[:, :, qoff:256] \
                        if qoff else pv.rearrange("p (h q) -> p h q", h=4)
                    E.do(lambda src=src, dst=dst: E.h.activation(
                        dst, src, AF.Exp, bias=ubias),
                        inc="ss", n=1, mark=("exp", s, i))


                def x2(t):
                    d = 60 + 24 * (t % 2)
                    E.wait("sv", ("ln2stats", s, t))
                    E.do(lambda d=d: E.h.activation(
                        stats[:, d + 2:d + 3], stats[:, d + 1:d + 2],
                        AF.Ln, bias=eps_sb[:, 0:1]))
                    E.do(lambda d=d: E.h.activation(
                        stats[:, d + 2:d + 3], stats[:, d + 2:d + 3],
                        AF.Exp, scale=-0.5),
                         inc="ss", n=1, mark=("rstd2", s, t))

                def x4(tp):
                    dp = 60 + 24 * (tp % 2)
                    E.wait("sv", ("ln4stats", s, tp))
                    E.do(lambda dp=dp: E.h.activation(
                        stats[:, dp + 18:dp + 19], stats[:, dp + 17:dp + 18],
                        AF.Ln, bias=eps_sb[:, 0:1]))
                    E.do(lambda dp=dp: E.h.activation(
                        stats[:, dp + 18:dp + 19], stats[:, dp + 18:dp + 19],
                        AF.Exp, scale=-0.5),
                         inc="ss", n=1, mark=("rstd4", s, tp))

                def x3(t):
                    d = 60 + 24 * (t % 2)
                    E.wait("sv", ("ln3stats", s, t))
                    E.do(lambda d=d: E.h.activation(
                        stats[:, d + 8:d + 9], stats[:, d + 5:d + 6],
                        AF.Ln, bias=eps_sb[:, 0:1]))
                    E.do(lambda d=d: E.h.activation(
                        stats[:, d + 8:d + 9], stats[:, d + 8:d + 9],
                        AF.Exp, scale=-0.5),
                         inc="ss", n=1, mark=("rstd3", s, t))

                def xcb(t):
                    E.wait("sv", ("quant", s, t))
                    E.wait("st", ("atT", s, t - 2))
                    E.do(lambda t=t: E.h.activation(
                        abf[:, (t % 2) * 1024:(t % 2 + 1) * 1024], qin8[:, :],
                        AF.Copy), inc="ss", n=1, mark=("castback", s, t))

                def p4sched(u0):
                    u1 = u0 + 1
                    out = []

                    def add(slot, fn, u, front):
                        if front and u < NT:
                            out.append((slot, lambda fn=fn, u=u: fn(u)))
                        elif not front and 1 <= u <= NT:
                            out.append((slot, lambda fn=fn, u=u: fn(u - 1)))
                    add(2, x2, u0, True)
                    add(5, x4, u0, False)
                    add(6, x3, u0, True)
                    add(8, xcb, u0, True)
                    add(11, x2, u1, True)
                    add(14, x4, u1, False)
                    add(15, x3, u1, True)
                    add(17, xcb, u1, True)
                    return out

                i = 0
                step = 0
                for qb in range(NQB):
                    pieces = []
                    if qb >= 1:
                        pieces = p4sched(step)
                        step += 2
                    pi = 0
                    k = 0
                    while i < len(ENTRIES) and ENTRIES[i]["qb"] == qb:
                        p3x(i, ENTRIES[i])
                        i += 1
                        k += 1
                        while pi < len(pieces) and pieces[pi][0] <= k - 4:
                            pieces[pi][1]()
                            pi += 1
                    while pi < len(pieces):
                        pieces[pi][1]()
                        pi += 1
                for u0 in (NT - 2, NT):
                    for _, fn in p4sched(u0):
                        fn()
                del p3x, p4sched, x2, x3, x4, xcb

        def stream_tensor(E):
            E.wait("sv", "usmall")
            E.do(lambda: E.h.matmul(pC[:, 0:1], onesrow[:, :], sm_sb[0:1, 3:4],
                                    start=True, stop=True, tile_position=(0, 0)))
            E.do(lambda: E.h.matmul(pC[:, 1:2], onesrow[:, :], ws_sb[0:1, 0:1],
                                    start=True, stop=True, tile_position=(0, 0)),
                 inc="st", n=1, mark="rank1a")
            E.wait("sv", "evac_a")
            E.do(lambda: E.h.matmul(pC[:, 0:512], onesrow[:, :],
                                    rowbuf[0:1, 0:512], start=True, stop=True,
                                    tile_position=(0, 0)),
                 inc="st", n=1, mark="rank1b")
            E.wait("sv", "evac_b")
            E.do(lambda: E.h.matmul(pC[:, 0:512], onesrow[:, :],
                                    rowbuf[0:1, 512:1024], start=True, stop=True,
                                    tile_position=(0, 0)),
                 inc="st", n=1, mark="rank1c")
            E.wait("ssm", "smalls")
            E.do(lambda: E.h.matmul(pA[:, 0:512], onesrow[:, :], fb_sb[0:1, 0:512],
                                    start=True, stop=True, tile_position=(0, 0)))
            E.do(lambda: E.h.matmul(pA[:, 512:1024], onesrow[:, :],
                                    fb_sb[0:1, 512:1024], start=True, stop=True,
                                    tile_position=(0, 0)),
                 inc="st", n=1, mark="rank1d")
            E.wait("sg", "consts")
            for j in range(8):
                E.wait("sv", ("decsel", j))
                if j >= 1:
                    E.wait("sv", ("wtevac", j - 1))
                for g in range(8):
                    E.do(lambda j=j, g=g: E.h.transpose(
                        pT[:, g * 128:(g + 1) * 128],
                        wn_bf[:, (j % 2) * 1024 + g * 128:(j % 2) * 1024 + (g + 1) * 128],
                        ident[:, :]),
                        inc=("st" if g == 7 else None), n=1,
                        mark=(("wtT", j) if g == 7 else None))

            for s in range(SEQS):

                def p2T(t):
                    sl = (t % 2) * 1024
                    trT = pT if t % 2 == 0 else pT2
                    E.wait("ss", ("ln1", s, t))
                    if t >= 2:
                        E.wait("ss", ("lnTevac", s, t - 2))
                    elif s > 0:
                        E.wait("sv", ("ffnev", s - 1, NT - 1) if t == 0
                               else ("ffo0", s - 1, NT - 1))
                    elif t == 0:
                        E.wait("sv", ("wtevac", 7))
                    for g in range(8):
                        E.do(lambda sl=sl, g=g, trT=trT: E.h.transpose(
                            trT[:, g * 128:(g + 1) * 128],
                            ln1[:, sl + g * 128:sl + (g + 1) * 128], ident[:, :]),
                            inc=("st" if g == 7 else None), n=1,
                            mark=(("lnTt", s, t) if g == 7 else None))

                def p2B1(t):
                    sl = (t % 2) * 1024
                    E.wait("ss", ("lnTevac", s, t))
                    if t > 0:
                        E.wait("ss", ("kvTevac", s, t - 1))
                    elif s > 0:
                        E.wait("ss", ("exp", s - 1, len(ENTRIES) - 1))
                        E.wait("sv", ("ot", s - 1, LAST_G, LAST_QB))
                    else:
                        E.wait("sv", "evac_d")
                    for g in range(8):
                        lhs = lnT[:, sl + g * 128:sl + (g + 1) * 128]
                        E.do(lambda g=g, lhs=lhs: E.h.matmul(
                            pC[:, 256:384], wkv_bf[:, g * 128:(g + 1) * 128], lhs,
                            start=(g == 0), stop=(g == 7)),
                            inc=("st" if g == 7 else None), n=1,
                            mark=(("kvmm", s, t) if g == 7 else None))
                    if t > 0:
                        E.wait("ss", ("qkvfree", s, t - 1))
                    for g in range(8):
                        lhs = lnT[:, sl + g * 128:sl + (g + 1) * 128]
                        E.do(lambda g=g, lhs=lhs: E.h.matmul(
                            pA[:, 0:512], lhs, wq_bf[:, g * 1024:g * 1024 + 512],
                            start=(g == 0), stop=(g == 7)))
                        E.do(lambda g=g, lhs=lhs: E.h.matmul(
                            pA[:, 512:1024], lhs,
                            wq_bf[:, g * 1024 + 512:(g + 1) * 1024],
                            start=(g == 0), stop=(g == 7)))

                def p2B2(t):
                    E.wait("ss", ("kvTevac", s, t))
                    E.do(lambda: E.h.matmul(pB[:, 0:512], kvT[:, :], wk_bf[:, :],
                                            start=True, stop=True))
                    E.do(lambda: E.h.matmul(pB[:, 512:1024], kvT[:, :], wv_bf[:, :],
                                            start=True, stop=True),
                         inc="st", n=1, mark=("kvvmm", s, t))

                def p2Q(t):
                    E.wait("sv", ("qpost", s, t))
                    E.wait("sv", ("kpost", s, t))
                    if t >= 1:
                        E.wait("sv", ("qkTevac", s, t - 1))
                    for j in range(4):
                        E.do(lambda j=j: E.h.transpose(
                            pDb[:, j * 128:(j + 1) * 128],
                            qhat[:, j * 128:(j + 1) * 128], ident[:, :]))
                        E.do(lambda j=j: E.h.transpose(
                            pDb[:, 512 + j * 128:512 + (j + 1) * 128],
                            khat[:, j * 128:(j + 1) * 128], ident[:, :]),
                            inc=("st" if j == 3 else None), n=1,
                            mark=(("qkT", s, t) if j == 3 else None))

                for u in range(NT + 2):
                    if u < NT:
                        p2T(u)
                    if 1 <= u <= NT:
                        p2B1(u - 1)
                        p2B2(u - 1)
                    if 2 <= u <= NT + 1:
                        p2Q(u - 2)
                del p2T, p2B1, p2B2, p2Q

                E.wait("sv", ("qkTevac", s, NT - 1))

                def emit_scores(i, en):
                    g, qb, c, qoff = en["g"], en["qb"], en["c"], en["qoff"]
                    st_ = (i % 2) * 1024
                    if i >= 2:
                        E.wait("ss", ("exp", s, i - 2))
                    for hp in (0, 2, 1, 3):
                        lhs = khT[32 * hp:32 * (hp + 1),
                                  g * T + c * 128: g * T + (c + 1) * 128]
                        rhs = qhT[32 * hp:32 * (hp + 1),
                                  g * T + qb * QB + qoff: g * T + (qb + 1) * QB]
                        dst = pS[:, st_ + hp * 256 + qoff:
                                 st_ + (hp + 1) * 256]
                        if hp in (0, 2):
                            E.do(lambda lhs=lhs, rhs=rhs, dst=dst, hp=hp:
                                 E.h.matmul(
                                     dst, lhs, rhs, start=True, stop=True,
                                     tile_position=(32 * hp, 0),
                                     skip_group_check=True),
                                 inc="stb", n=1,
                                 mark=(("scp", s, i) if hp == 2 else None))
                        else:
                            if hp == 1:
                                E.wait("stb", ("scp", s, i))
                            E.do(lambda lhs=lhs, rhs=rhs, dst=dst, hp=hp:
                                 E.h.matmul(
                                     dst, lhs, rhs, start=True, stop=True,
                                     tile_position=(32 * hp, 0),
                                     skip_group_check=True),
                                 inc=("st" if hp == 3 else None), n=1,
                                 mark=(("sc", s, i) if hp == 3 else None))

                def emit_pv(i, en):
                    g, qb, c, qoff = en["g"], en["qb"], en["c"], en["qoff"]
                    slot = i % 4
                    ri = qb * NG + g
                    roff = 0
                    E.wait("ss", ("exp", s, i))
                    if en["diag"]:
                        E.wait("sv", ("mask", s, i))
                    if en["first"]:
                        if ri >= 1:
                            E.wait("sv", ("ot", s, (ri - 1) % NG,
                                          (ri - 1) // NG))
                        else:
                            E.wait("sv", ("pvz", s))
                    pv = pt_ring[:, slot * 1024:(slot + 1) * 1024]
                    for hp in range(4):
                        h = 4 * g + hp
                        vlhs = vtok[:, c * 512 + 32 * h: c * 512 + 32 * (h + 1)]
                        rhs = pv[:, hp * 256 + qoff:(hp + 1) * 256]
                        E.do(lambda vlhs=vlhs, rhs=rhs, hp=hp, en=en, qoff=qoff,
                             roff=roff: E.h.matmul(
                                 pC[32 * hp:32 * (hp + 1),
                                    roff + qoff:roff + 256], vlhs, rhs,
                                 start=False, stop=en["last"],
                                 tile_position=(0, 32 * hp), skip_group_check=True))
                    for hp in range(4):
                        rhs = pv[:, hp * 256 + qoff:(hp + 1) * 256]
                        E.do(lambda rhs=rhs, hp=hp, en=en, qoff=qoff, roff=roff:
                             E.h.matmul(
                                 pD[32 * hp:32 * (hp + 1),
                                    roff + qoff:roff + 256],
                                 ones32[:, :], rhs, start=False, stop=en["last"],
                                 tile_position=(0, 32 * hp), skip_group_check=True),
                             inc=("st" if hp == 3 else None), n=1,
                             mark=(("pv", s, i) if hp == 3 else None))

                def tW0(t):
                    E.wait("sv", ("ot", s, NG - 1, t // 2))
                    if t >= 2:
                        E.wait("sv", ("ffo0", s, t - 2))
                    elif t == 1:
                        E.wait("sv", ("aoev0", s, 0))
                    else:
                        E.wait("sv", ("qkTevac", s, NT - 1))
                    for g in range(4):
                        lhs = otf[:, g * T + t * 128: g * T + (t + 1) * 128]
                        E.do(lambda g=g, lhs=lhs: E.h.matmul(
                            pW[:, 0:512], lhs,
                            wo_bf[:, g * 1024:g * 1024 + 512],
                            start=(g == 0), stop=(g == 3)),
                            inc=("st" if g == 3 else None), n=1,
                            mark=(("womm", s, t, 0) if g == 3 else None))

                def tW1(t):
                    if t >= 2:
                        E.wait("sv", ("ffnev", s, t - 2))
                    elif t == 1:
                        E.wait("sv", ("aoev", s, 0))
                    else:
                        E.wait("sv", ("qkTevac", s, NT - 2))
                    for g in range(4):
                        lhs = otf[:, g * T + t * 128: g * T + (t + 1) * 128]
                        E.do(lambda g=g, lhs=lhs: E.h.matmul(
                            pTf[:, 0:512], lhs,
                            wo_bf[:, g * 1024 + 512:(g + 1) * 1024],
                            start=(g == 0), stop=(g == 3)),
                            inc=("st" if g == 3 else None), n=1,
                            mark=(("womm", s, t, 1) if g == 3 else None))

                def tAT(tp):
                    slp = (tp % 2) * 1024
                    E.wait("ss", ("castback", s, tp))
                    E.wait("sv", ("aoev", s, tp + 1) if tp + 1 < NT else
                           ("ffnev", s, tp - 1))
                    for g in range(8):
                        E.do(lambda slp=slp, g=g: E.h.transpose(
                            pT[:, g * 128:(g + 1) * 128],
                            abf[:, slp + g * 128:slp + (g + 1) * 128],
                            ident[:, :]),
                            inc=("st" if g == 7 else None), n=1,
                            mark=(("atT", s, tp) if g == 7 else None))

                def tF0(tp):
                    E.wait("sv", ("atevac", s, tp))
                    E.wait("sv", ("aoev0", s, tp + 1) if tp + 1 < NT else
                           ("ffo0", s, tp - 1))
                    slp = (tp % 2) * 1024
                    for g in range(8):
                        lhs = atT[:, slp + g * 128:slp + (g + 1) * 128]
                        E.do(lambda g=g, lhs=lhs: E.h.matmul(
                            pW[:, 0:512], lhs,
                            wt_bf[:, g * 1024:g * 1024 + 512],
                            start=(g == 0), stop=(g == 7)),
                            inc=("st" if g == 7 else None), n=1,
                            mark=(("ffnmm", s, tp, 0) if g == 7 else None))

                def tF1(tp):
                    slp = (tp % 2) * 1024
                    for g in range(8):
                        lhs = atT[:, slp + g * 128:slp + (g + 1) * 128]
                        E.do(lambda g=g, lhs=lhs: E.h.matmul(
                            pTf[:, 0:512], lhs,
                            wt_bf[:, g * 1024 + 512:(g + 1) * 1024],
                            start=(g == 0), stop=(g == 7)),
                            inc=("st" if g == 7 else None), n=1,
                            mark=(("ffnmm", s, tp, 1) if g == 7 else None))

                def p4sched(u0):
                    u1 = u0 + 1
                    out = []

                    def add(slot, fn, u, front):
                        if front and u < NT:
                            out.append((slot, lambda fn=fn, u=u: fn(u)))
                        elif not front and 1 <= u <= NT:
                            out.append((slot, lambda fn=fn, u=u: fn(u - 1)))
                    add(0, tW0, u0, True)
                    add(1, tW1, u0, True)
                    add(2, tAT, u0, False)
                    add(3, tF0, u0, False)
                    add(4, tF1, u0, False)
                    add(9, tW0, u1, True)
                    add(10, tW1, u1, True)
                    add(11, tAT, u1, False)
                    add(12, tF0, u1, False)
                    add(13, tF1, u1, False)
                    return out

                i = 0
                step = 0
                for qb in range(NQB):
                    pieces = []
                    if qb >= 1:
                        pieces = p4sched(step)
                        step += 2
                    pi = 0
                    k = 0
                    while i < len(ENTRIES) and ENTRIES[i]["qb"] == qb:
                        emit_scores(i, ENTRIES[i])
                        if i >= 2:
                            emit_pv(i - 2, ENTRIES[i - 2])
                        i += 1
                        k += 1
                        while pi < len(pieces) and pieces[pi][0] <= k - 2:
                            pieces[pi][1]()
                            pi += 1
                    while pi < len(pieces):
                        pieces[pi][1]()
                        pi += 1
                emit_pv(len(ENTRIES) - 2, ENTRIES[-2])
                emit_pv(len(ENTRIES) - 1, ENTRIES[-1])
                for u0 in (NT - 2, NT):
                    for _, fn in p4sched(u0):
                        fn()
                del emit_scores, emit_pv, p4sched, tW0, tW1, tAT, tF0, tF1

        streams = dict(sync=stream_sync, gpsimd=stream_gpsimd,
                       vector=stream_vector, scalar=stream_scalar,
                       tensor=stream_tensor)
        for name, fn in streams.items():
            fn(Eng(sched, None, sems, dry=True))
        block.sync(lambda eng: streams["sync"](Eng(sched, eng, sems, dry=False)))
        block.gpsimd(lambda eng: streams["gpsimd"](Eng(sched, eng, sems, dry=False)))
        block.vector(lambda eng: streams["vector"](Eng(sched, eng, sems, dry=False)))
        block.scalar(lambda eng: streams["scalar"](Eng(sched, eng, sems, dry=False)))
        block.tensor(lambda eng: streams["tensor"](Eng(sched, eng, sems, dry=False)))

    return nc


_NC_CACHE = None


def kernel(L0_attn_x, L0_attn_Wq, L0_attn_Wkv, L0_attn_Wk, L0_attn_Wv,
           L0_attn_Wo, L0_attn_q_g, L0_attn_k_g, L0_ffn_weight,
           L0_ffn_weight_scale, L0_ffn_bias):
    global _NC_CACHE
    x = np.ascontiguousarray(np.asarray(L0_attn_x, dtype=np.float32))
    common = {
        "wq": np.ascontiguousarray(np.asarray(L0_attn_Wq, dtype=np.float32)),
        "wkv": np.ascontiguousarray(np.asarray(L0_attn_Wkv, dtype=np.float32)),
        "wk": np.ascontiguousarray(np.asarray(L0_attn_Wk, dtype=np.float32)),
        "wv": np.ascontiguousarray(np.asarray(L0_attn_Wv, dtype=np.float32)),
        "wo": np.ascontiguousarray(np.asarray(L0_attn_Wo, dtype=np.float32)),
        "qg": np.asarray(L0_attn_q_g, dtype=np.float32).reshape(1, HD),
        "kg": np.asarray(L0_attn_k_g, dtype=np.float32).reshape(1, ROPE),
        "w8": np.ascontiguousarray(np.asarray(L0_ffn_weight).view(np.uint8)),
        "ws": np.asarray(L0_ffn_weight_scale, dtype=np.float32).reshape(1, 1),
        "fb": np.asarray(L0_ffn_bias, dtype=np.float32).reshape(1, D),
    }
    if _NC_CACHE is None:
        _NC_CACHE = build_core()
    in_maps = []
    for c in range(NCORES):
        m = dict(common)
        m["x"] = x[c * SEQS:(c + 1) * SEQS]
        in_maps.append(m)
    res = run_bass_kernel_spmd(_NC_CACHE, in_maps, core_ids=list(range(NCORES)))
    return np.concatenate([r["out"] for r in res.results], axis=0)


# version 23 (final)
# speedup vs baseline: 1.0421x; 1.0026x over previous
import numpy as np
import ml_dtypes
from contextlib import ExitStack

import concourse.bass as bass
import concourse.mybir as mybir
from concourse.bass_utils import run_bass_kernel_spmd

F32 = mybir.dt.float32
BF16 = mybir.dt.bfloat16
FP8 = mybir.dt.float8e4
U8 = mybir.dt.uint8
AF = mybir.ActivationFunctionType
ALU = mybir.AluOpType
AX = mybir.AxisListType

NCORES = 8
SEQS = 2
T = 2048
D = 1024
H = 16
HD = 64
ROPE = 32
LAT = 128
NT = T // 128
QB = 256
NQB = T // QB
NG = 4
EPS_LN = 1e-5
EPS_RMS = 1e-6


def chunk_entries():
    out = []
    for qb in range(NQB):
        for g in range(NG):
            nch = 2 * qb + 2
            for c in range(nch):
                out.append(dict(qb=qb, g=g, c=c,
                                qoff=(128 if c == 2 * qb + 1 else 0),
                                first=(c == 0), last=(c == nch - 1),
                                diag=(c >= 2 * qb)))
    return out


ENTRIES = chunk_entries()
LAST_G, LAST_QB = ENTRIES[-1]["g"], ENTRIES[-1]["qb"]


class Sched:
    def __init__(self):
        self.counts = {}
        self.marks = {}

    def inc(self, sem, n):
        self.counts[sem] = self.counts.get(sem, 0) + n

    def mark(self, sem, key):
        self.marks[(sem, key)] = self.counts.get(sem, 0)

    def val(self, sem, key):
        return self.marks.get((sem, key), 0)


class Eng:
    def __init__(self, sched, handle, sems, dry):
        self.s = sched
        self.h = handle
        self.sems = sems
        self.dry = dry
        self.gates = []

    def push_gate(self, b):
        self.gates.append(b)

    def pop_gate(self):
        self.gates.pop()

    def do(self, fn, inc=None, n=1, mark=None):
        if self.gates and not all(self.gates):
            return
        if self.dry:
            if inc is not None:
                self.s.inc(inc, n)
                if mark is not None:
                    self.s.mark(inc, mark)
            return
        instr = fn()
        if inc is not None:
            instr.then_inc(self.sems[inc], n)

    def dobar(self, fn):
        self.svb_count = getattr(self, "svb_count", 0) + 1
        self.do(fn, inc="svb", n=1)

    def waitbar(self):
        if self.gates and not all(self.gates):
            return
        if self.dry:
            return
        if getattr(self, "svb_count", 0) > 0:
            self.h.wait_ge(self.sems["svb"], self.svb_count)

    def wait(self, sem, key):
        if self.gates and not all(self.gates):
            return
        if self.dry:
            return
        v = self.s.val(sem, key)
        if v > 0:
            self.h.wait_ge(self.sems[sem], v)


def build_core(phases=99):
    pass_seqs = SEQS if phases >= 4 else 1

    nc = bass.Bass("TRN2", target_bir_lowering=False,
                   detect_race_conditions=False)

    x_in = nc.dram_tensor("x", [SEQS, T, D], F32, kind="ExternalInput").ap()
    wq_in = nc.dram_tensor("wq", [D, D], F32, kind="ExternalInput").ap()
    wkv_in = nc.dram_tensor("wkv", [D, LAT], F32, kind="ExternalInput").ap()
    wk_in = nc.dram_tensor("wk", [LAT, 512], F32, kind="ExternalInput").ap()
    wv_in = nc.dram_tensor("wv", [LAT, 512], F32, kind="ExternalInput").ap()
    wo_in = nc.dram_tensor("wo", [D, D], F32, kind="ExternalInput").ap()
    qg_in = nc.dram_tensor("qg", [1, HD], F32, kind="ExternalInput").ap()
    kg_in = nc.dram_tensor("kg", [1, ROPE], F32, kind="ExternalInput").ap()
    w8_in = nc.dram_tensor("w8", [D, D], U8, kind="ExternalInput").ap()
    ws_in = nc.dram_tensor("ws", [1, 1], F32, kind="ExternalInput").ap()
    fb_in = nc.dram_tensor("fb", [1, D], F32, kind="ExternalInput").ap()
    out = nc.dram_tensor("out", [SEQS, T, D], F32, kind="ExternalOutput").ap()

    sched = Sched()
    do_p3 = phases >= 2
    do_p4 = phases >= 3
    nseq = SEQS if phases >= 4 else (1 if phases < 4 else SEQS)


    with ExitStack() as st:
        e = st.enter_context

        wq_bf = e(nc.sbuf_tensor("wq_bf", [128, 8 * 1024], BF16))
        wkv_bf = e(nc.sbuf_tensor("wkv_bf", [128, 8 * 128], BF16))
        wk_bf = e(nc.sbuf_tensor("wk_bf", [128, 512], BF16))
        wv_bf = e(nc.sbuf_tensor("wv_bf", [128, 512], BF16))
        wo_bf = e(nc.sbuf_tensor("wo_bf", [128, 4 * 1024], BF16))
        wt_bf = e(nc.sbuf_tensor("wt_bf", [128, 8 * 1024], BF16))
        bias_bc = e(nc.sbuf_tensor("bias_bc", [128, 1024], F32))
        qg8_bc = e(nc.sbuf_tensor("qg8_bc", [128, 512], BF16))
        kg_bc = e(nc.sbuf_tensor("kg_bc", [128, 512], BF16))
        ident = e(nc.sbuf_tensor("ident", [128, 128], BF16))
        tri = e(nc.sbuf_tensor("tri", [128, 128], BF16))
        ones32 = e(nc.sbuf_tensor("ones32", [128, 32], BF16))
        onesrow = e(nc.sbuf_tensor("onesrow", [1, 128], F32))
        ubws = e(nc.sbuf_tensor("ubws", [128, 2], F32))
        ubias = ubws[:, 0:1]
        ws128 = ubws[:, 1:2]
        qg_sb = e(nc.sbuf_tensor("qg_sb", [1, HD], F32))
        kg_sb = e(nc.sbuf_tensor("kg_sb", [1, ROPE], F32))
        ws_sb = e(nc.sbuf_tensor("ws_sb", [1, 1], F32))
        fb_sb = e(nc.sbuf_tensor("fb_sb", [1, D], F32))
        sm_sb = e(nc.sbuf_tensor("sm_sb", [1, 8], F32))
        eps_sb = e(nc.sbuf_tensor("eps_sb", [128, 1], F32))
        zrow = e(nc.sbuf_tensor("zrow", [128, 256], BF16))


        qhT = e(nc.sbuf_tensor("qhT", [128, NG * T], BF16))
        khT = e(nc.sbuf_tensor("khT", [128, NG * T], BF16))
        vtok = e(nc.sbuf_tensor("vtok", [128, NT * 512], BF16))
        otf = e(nc.sbuf_tensor("otf", [128, NG * T], BF16))
        pt_ring = e(nc.sbuf_tensor("pt_ring", [128, 4 * 1024], BF16))

        xf = e(nc.sbuf_tensor("xf", [128, 2 * 1024], F32))
        xb = e(nc.sbuf_tensor("xb", [128, 2 * 1024], F32))
        ln1 = e(nc.sbuf_tensor("ln1", [128, 2 * 1024], BF16))
        lnT = e(nc.sbuf_tensor("lnT", [128, 2 * 1024], BF16))
        qf = e(nc.sbuf_tensor("qf", [128, 1024], F32))
        qsq = e(nc.sbuf_tensor("qsq", [128, 1024], F32))
        qhat = e(nc.sbuf_tensor("qhat", [128, 512], BF16))
        khat = e(nc.sbuf_tensor("khat", [128, 512], BF16))
        kvsb = e(nc.sbuf_tensor("kvsb", [128, 128], BF16))
        kvT = e(nc.sbuf_tensor("kvT", [128, 128], BF16))
        rtmp = e(nc.sbuf_tensor("rtmp", [128, 512], F32))
        stats = e(nc.sbuf_tensor("stats", [128, 128], F32))
        rq = e(nc.sbuf_tensor("rq", [128, 32], F32))
        sums_g = e(nc.sbuf_tensor("sums_g", [128, 256], F32))
        sums_r = e(nc.sbuf_tensor("sums_r", [128, 256], F32))
        aosb = e(nc.sbuf_tensor("aosb", [128, 1024], F32))
        xmid = e(nc.sbuf_tensor("xmid", [128, 2 * 1024], F32))
        xadj = e(nc.sbuf_tensor("xadj", [128, 1024], F32))
        qin8 = e(nc.sbuf_tensor("qin8", [128, 1024], FP8))
        abf = e(nc.sbuf_tensor("abf", [128, 2 * 1024], BF16))
        atT = e(nc.sbuf_tensor("atT", [128, 2 * 1024], BF16))
        ffo = e(nc.sbuf_tensor("ffo", [128, 1024], F32))
        xout = e(nc.sbuf_tensor("xout", [128, 2 * 1024], F32))

        pS = e(nc.psum_tensor("pS", [128, 2048], F32))
        pA = pS[:, 0:1024]
        pB = pS[:, 1024:2048]
        pC = e(nc.psum_tensor("pC", [128, 512], F32))
        pT = e(nc.psum_tensor("pT", [128, 1024], BF16))
        pW = e(nc.psum_tensor("pW", [128, 512], F32))
        pT2 = pW.bitcast(BF16)
        pTf = pT.bitcast(F32)
        pD = e(nc.psum_tensor("pD", [128, 512], F32))
        pDb = pD.bitcast(BF16)

        xmid_u8 = xmid[:, :].bitcast(U8)
        dec_u = xmid_u8[:, 0:1024]
        dec_a = xmid_u8[:, 1024:2048]
        dec_b = xmid_u8[:, 2048:3072]
        dec_sg = xmid_u8[:, 3072:4096]
        dec_m = xmid_u8[:, 4096:5120]
        dec_A = abf[:, 0:1024]
        dec_B = abf[:, 1024:2048]
        wn_bf = atT
        w8_st = xout[:, :].bitcast(U8)
        rowbuf = qsq

        sems = {n: e(nc.semaphore(n)) for n in
                ["svb", "stb", "st", "sv", "ss", "sg", "ssm", "swq0", "swq1", "swo0",
                 "swo1", "skv", "swk", "swv", "sw80", "sw81", "sx0", "sx1",
                 "sb0", "sb1", "so0", "so1"]}
        block = e(nc.Block())


        def stream_sync(E):
            E.do(lambda: E.h.dma_start(qg_sb[:, :], qg_in), inc="ssm", n=16)
            E.do(lambda: E.h.dma_start(kg_sb[:, :], kg_in), inc="ssm", n=16)
            E.do(lambda: E.h.dma_start(ws_sb[:, :], ws_in), inc="ssm", n=16)
            E.do(lambda: E.h.dma_start(fb_sb[:, :], fb_in), inc="ssm", n=16,
                 mark="smalls")
            for g in range(8):
                E.wait("sv", ("wqcast", g - 2))
                E.do(lambda g=g: E.h.dma_start(
                    xf[:, (g % 2) * 1024:(g % 2 + 1) * 1024],
                    wq_in[g * 128:(g + 1) * 128, :]), inc="swq%d" % (g % 2),
                    n=16, mark=("wq", g))
            for g in range(4):
                E.wait("sv", ("wocast", g - 2))
                src = bass.AP(wo_in.tensor, (4 * g) * 64 * 1024,
                              [[64 * 1024, 4], [1024, 32], [1, 1024]])
                E.do(lambda g=g, src=src: E.h.dma_start(
                    xb[:, (g % 2) * 1024:(g % 2 + 1) * 1024], src),
                    inc="swo%d" % (g % 2), n=16, mark=("wo", g))
            E.wait("sv", ("wqcast", 7))
            E.do(lambda: E.h.dma_start(
                xf[:, 0:1024].rearrange("p (a b) -> p a b", a=8),
                wkv_in.rearrange("(c p) l -> p c l", p=128)), inc="skv", n=16,
                mark="wkv")
            E.wait("sv", ("wocast", 3))
            E.do(lambda: E.h.dma_start(xb[:, 0:512], wk_in), inc="swk", n=16,
                 mark="wk")
            E.do(lambda: E.h.dma_start(xb[:, 512:1024], wv_in), inc="swv", n=16,
                 mark="wv")
            for j in range(8):
                E.wait("sv", ("decsel", j - 2))
                E.do(lambda j=j: E.h.dma_start(
                    w8_st[:, (j % 2) * 1024:(j % 2 + 1) * 1024],
                    w8_in[j * 128:(j + 1) * 128, :]), inc="sw8%d" % (j % 2),
                    n=16, mark=("w8", j))
            for s in range(SEQS):
                for t in range(NT):
                    if s == 0 and t < 2:
                        E.wait("sv", "wkvcast")
                    else:
                        E.wait("ss", ("ln1", s, t - 2) if t >= 2 else
                               ("ln1", s - 1, NT + t - 2))
                    E.do(lambda s=s, t=t: E.h.dma_start(
                        xf[:, (t % 2) * 1024:(t % 2 + 1) * 1024],
                        x_in[s, t * 128:(t + 1) * 128, :]),
                        inc="sx%d" % (t % 2), n=16, mark=("x", s, t))
                for t in range(NT):
                    if s == 0 and t < 2:
                        E.wait("sv", "wkwvcast")
                    else:
                        E.wait("sv", ("ln2res", s, t - 2) if t >= 2 else
                               ("ln2res", s - 1, NT + t - 2))
                    E.do(lambda s=s, t=t: E.h.dma_start(
                        xb[:, (t % 2) * 1024:(t % 2 + 1) * 1024],
                        x_in[s, t * 128:(t + 1) * 128, :]),
                        inc="sb%d" % (t % 2), n=16, mark=("xb", s, t))

        def stream_gpsimd(E):
            E.do(lambda: E.h.memset(ident[:, :], 0.0), inc="sg", n=1,
                 mark="c_mset")
            E.do(lambda: E.h.memset(tri[:, :], 1.0))
            E.do(lambda: E.h.memset(ones32[:, :], 1.0))
            E.do(lambda: E.h.memset(onesrow[:, :], 1.0))
            E.do(lambda: E.h.memset(eps_sb[:, :], EPS_LN))
            E.do(lambda: E.h.memset(zrow[:, :], 0.0), inc="sg", n=1,
                 mark="c_mset2")
            E.wait("sg", "c_mset2")
            E.do(lambda: E.h.affine_select(
                out=ident[:, :], in_=ident[:, :], compare_op=ALU.not_equal,
                fill=1.0, base=0, pattern=[[-1, 128]], channel_multiplier=1))
            E.do(lambda: E.h.affine_select(
                out=tri[:, :], in_=tri[:, :], compare_op=ALU.is_ge,
                fill=0.0, base=0, pattern=[[1, 128]], channel_multiplier=-1),
                inc="sg", n=1, mark="consts")
            for s in range(SEQS):
                for t in range(NT):
                    E.wait("sv", ("xout", s, t))
                    E.do(lambda s=s, t=t: E.h.dma_start(
                        out[s, t * 128:(t + 1) * 128, :],
                        xout[:, (t % 2) * 1024:(t % 2 + 1) * 1024]),
                        inc="so%d" % (t % 2), n=16, mark=("out", s, t))
            E.wait("so0", ("out", SEQS - 1, NT - 2))
            E.wait("so1", ("out", SEQS - 1, NT - 1))

        def combine_ln(E, src, dst, inv_n, inc=None, mark=None):
            alu = ALU
            E.do(lambda: E.h.bn_aggr(stats[:, 12:14], stats[:, src:src + 6]))
            E.dobar(lambda: E.h.bn_aggr(stats[:, 14:16], stats[:, src + 6:src + 12]))
            E.waitbar()
            E.do(lambda: E.h.tensor_scalar(
                stats[:, dst:dst + 1], stats[:, 12:13], stats[:, 14:15], 0.5,
                alu.add, alu.mult))
            E.dobar(lambda: E.h.tensor_scalar(
                stats[:, 124:125], stats[:, 12:13], stats[:, 14:15], None,
                alu.subtract))
            E.waitbar()
            E.do(lambda: E.h.tensor_scalar(
                stats[:, 125:126], stats[:, 124:125], stats[:, 124:125], 0.25,
                alu.mult, alu.mult))
            E.dobar(lambda: E.h.tensor_scalar(
                stats[:, dst + 1:dst + 2], stats[:, 13:14], stats[:, 15:16], 0.5,
                alu.add, alu.mult))
            E.waitbar()
            E.do(lambda: E.h.tensor_scalar(
                stats[:, dst + 1:dst + 2], stats[:, dst + 1:dst + 2],
                stats[:, 125:126], None, alu.add), inc=inc, n=1, mark=mark)

        def stream_vector(E):
            alu = ALU
            for g in range(8):
                E.wait("swq%d" % (g % 2), ("wq", g))
                E.do(lambda g=g: E.h.tensor_copy(
                    wq_bf[:, g * 1024:(g + 1) * 1024],
                    xf[:, (g % 2) * 1024:(g % 2 + 1) * 1024]),
                    inc="sv", n=1, mark=("wqcast", g))
            for g in range(4):
                E.wait("swo%d" % (g % 2), ("wo", g))
                E.do(lambda g=g: E.h.tensor_copy(
                    wo_bf[:, g * 1024:(g + 1) * 1024],
                    xb[:, (g % 2) * 1024:(g % 2 + 1) * 1024]),
                    inc="sv", n=1, mark=("wocast", g))
            E.wait("skv", "wkv")
            E.do(lambda: E.h.tensor_copy(wkv_bf[:, :], xf[:, 0:1024]),
                 inc="sv", n=1, mark="wkvcast")
            E.wait("swk", "wk")
            E.do(lambda: E.h.tensor_copy(wk_bf[:, :], xb[:, 0:512]))
            E.wait("swv", "wv")
            E.do(lambda: E.h.tensor_copy(wv_bf[:, :], xb[:, 512:1024]),
                 inc="sv", n=1, mark="wkwvcast")
            E.wait("ssm", "smalls")
            E.do(lambda: E.h.tensor_scalar(
                rowbuf[0:1, 0:512].rearrange("p (h r) -> p h r", h=16),
                qg_sb[0:1, 0:ROPE].unsqueeze(1).broadcast_to([1, 16, ROPE]),
                0.125, None, alu.mult))
            E.dobar(lambda: E.h.tensor_copy(
                rowbuf[0:1, 512:1024].rearrange("p (h r) -> p h r", h=16),
                kg_sb[0:1, 0:ROPE].unsqueeze(1).broadcast_to([1, 16, ROPE])))
            E.waitbar()
            E.do(lambda: E.h.tensor_reduce(
                sm_sb[0:1, 0:1].unsqueeze(2), rowbuf[0:1, 0:32].unsqueeze(1),
                AX.X, alu.max, apply_absolute_value=True))
            E.dobar(lambda: E.h.tensor_reduce(
                sm_sb[0:1, 1:2].unsqueeze(2), kg_sb[0:1, 0:ROPE].unsqueeze(1),
                AX.X, alu.max, apply_absolute_value=True))
            E.waitbar()
            E.do(lambda: E.h.tensor_scalar(
                sm_sb[0:1, 3:4], sm_sb[0:1, 0:1], sm_sb[0:1, 1:2],
                -22.627416997969522, alu.mult, alu.mult),
                inc="sv", n=1, mark="usmall")
            E.wait("st", "rank1a")
            E.do(lambda: E.h.tensor_copy(ubws[:, :], pC[:, 0:2]),
                 inc="sv", n=1, mark="evac_a")
            E.wait("st", "rank1b")
            E.do(lambda: E.h.tensor_copy(qg8_bc[:, :], pC[:, 0:512]),
                 inc="sv", n=1, mark="evac_b")
            E.wait("st", "rank1c")
            E.do(lambda: E.h.tensor_copy(kg_bc[:, :], pC[:, 0:512]))
            E.do(lambda: E.h.tensor_tensor(kg_bc[:, :], kg_bc[:, :],
                                           qg8_bc[:, :], alu.mult),
                 inc="sv", n=1, mark="evac_c")
            E.wait("st", "rank1d")
            E.do(lambda: E.h.tensor_copy(bias_bc[:, :], pA[:, :]),
                 inc="sv", n=1, mark="evac_d")
            for j in range(8):
                E.wait("sw8%d" % (j % 2), ("w8", j))
                w8c = w8_st[:, (j % 2) * 1024:(j % 2 + 1) * 1024]
                E.do(lambda w8c=w8c: E.h.tensor_scalar(
                    dec_u, w8c, 0x7F, None, alu.bitwise_and))
                E.do(lambda w8c=w8c: E.h.tensor_scalar(
                    dec_sg, w8c, 0x80, None, alu.bitwise_and))
                E.do(lambda: E.h.tensor_scalar(
                    dec_a, dec_u, 0x77, None, alu.min))
                E.do(lambda: E.h.tensor_scalar(
                    dec_b, dec_u, 16, 8, alu.max, alu.subtract))
                E.do(lambda: E.h.tensor_scalar(
                    dec_m, dec_u, 0x78, None, alu.is_ge))
                E.do(lambda: E.h.tensor_tensor(
                    dec_a, dec_a, dec_sg, alu.bitwise_or))
                E.do(lambda: E.h.tensor_tensor(
                    dec_b, dec_b, dec_sg, alu.bitwise_or),
                    inc="sv", n=1, mark=("decbits", j))
                E.wait("ss", ("deccast", j))
                if j >= 2:
                    E.wait("st", ("wtT", j - 2))
                E.do(lambda j=j: E.h.select(
                    wn_bf[:, (j % 2) * 1024:(j % 2 + 1) * 1024],
                    dec_m, dec_B, dec_A),
                    inc="sv", n=1, mark=("decsel", j))
                E.wait("st", ("wtT", j))
                E.do(lambda j=j: E.h.tensor_copy(
                    wt_bf[:, :].rearrange("p (g n) -> p g n", g=8)[:, :, j * 128:(j + 1) * 128],
                    pT[:, :].rearrange("p (g n) -> p g n", g=8)),
                    inc="sv", n=1, mark=("wtevac", j))

            for s in range(SEQS):

                def p1A(t):
                    sl = (t % 2) * 1024
                    d = 16 + 40 * (t % 2)
                    xt = xf[:, sl:sl + 1024]
                    E.wait("sx%d" % (t % 2), ("x", s, t))
                    E.do(lambda xt=xt: E.h.bn_stats(stats[:, 0:6], xt[:, 0:512]))
                    E.do(lambda xt=xt: E.h.bn_stats(stats[:, 6:12], xt[:, 512:1024]))
                    combine_ln(E, 0, d, 1.0 / 1024)
                    E.do(lambda d=d: E.h.tensor_scalar(
                        stats[:, d + 3:d + 4], stats[:, d:d + 1], -1.0, None,
                        alu.mult), inc="sv", n=1, mark=("ln1stats", s, t))

                def p1E(t):
                    E.wait("ss", ("qfev", s, t))
                    E.do(lambda: E.h.tensor_tensor(qsq[:, :], qf[:, :], qf[:, :],
                                                   alu.mult))
                    E.do(lambda: E.h.tensor_reduce(
                        stats[:, 20:36].unsqueeze(2),
                        qsq[:, :].rearrange("p (h d) -> p h d", h=16), AX.X,
                        alu.add))
                    E.wait("ss", ("qkvfree", s, t))
                    E.do(lambda: E.h.tensor_tensor(
                        qsq[:, 0:512], xadj[:, 0:512], xadj[:, 0:512], alu.mult))
                    E.dobar(lambda: E.h.tensor_reduce(
                        stats[:, 36:52].unsqueeze(2),
                        qsq[:, 0:512].rearrange("p (h d) -> p h d", h=16),
                        AX.X, alu.add))
                    E.waitbar()
                    E.do(lambda: E.h.tensor_scalar(
                        stats[:, 20:36], stats[:, 20:36], 1.0 / 64, EPS_RMS,
                        alu.mult, alu.add))
                    E.do(lambda: E.h.tensor_scalar(
                        stats[:, 36:52], stats[:, 36:52], 1.0 / 32, EPS_RMS,
                        alu.mult, alu.add),
                        inc="sv", n=1, mark=("rmsstage", s, t))
                    E.wait("ss", ("rmsr", s, t))
                    E.do(lambda: E.h.tensor_tensor(
                        qhat[:, :].rearrange("p (h r) -> p h r", h=16),
                        qf[:, :].rearrange("p (h d) -> p h d", h=16)[:, :, 0:ROPE],
                        rq[:, 0:16].unsqueeze(2).broadcast_to([128, 16, ROPE]),
                        alu.mult),
                        inc="sv", n=1, mark=("qpost", s, t))
                    E.do(lambda: E.h.tensor_tensor(
                        rtmp[:, :].rearrange("p (h r) -> p h r", h=16),
                        xadj[:, 0:512].rearrange("p (h d) -> p h d", h=16),
                        rq[:, 16:32].unsqueeze(2).broadcast_to([128, 16, ROPE]),
                        alu.mult))
                    E.do(lambda: E.h.tensor_tensor(
                        khat[:, :], rtmp[:, :], kg_bc[:, :], alu.mult),
                        inc="sv", n=1, mark=("kpost", s, t))

                def p1Q(t):
                    E.wait("st", ("qkT", s, t))
                    E.do(lambda t=t: E.h.tensor_copy(
                        qhT[:, :].rearrange("p (g n) -> p g n", g=NG)[:, :, t * 128:(t + 1) * 128],
                        pDb[:, 0:512].rearrange("p (g n) -> p g n", g=NG)))
                    E.do(lambda t=t: E.h.tensor_copy(
                        khT[:, :].rearrange("p (g n) -> p g n", g=NG)[:, :, t * 128:(t + 1) * 128],
                        pDb[:, 512:1024].rearrange("p (g n) -> p g n", g=NG)),
                        inc="sv", n=1, mark=("qkTevac", s, t))

                for u in range(NT + 2):
                    if u == 0:
                        p1A(0)
                    if u + 1 < NT:
                        p1A(u + 1)
                    if 2 <= u <= NT + 1:
                        p1E(u - 2)
                        p1Q(u - 2)
                del p1A, p1E, p1Q
                E.do(lambda: E.h.tensor_copy(pC[:, 0:256], zrow[:, :]))
                E.do(lambda: E.h.tensor_copy(pD[:, 0:256], zrow[:, :]),
                     inc="sv", n=1, mark=("pvz", s))

                def p3e(i, en):
                    slot = i % 4
                    if en["diag"]:
                        E.wait("ss", ("exp", s, i))
                        coff = en["qoff"]
                        pv = pt_ring[:, slot * 1024:(slot + 1) * 1024]
                        E.do(lambda pv=pv, coff=coff: E.h.tensor_tensor(
                            pv.rearrange("p (h q) -> p h q", h=4)[:, :, coff:coff + 128],
                            pv.rearrange("p (h q) -> p h q", h=4)[:, :, coff:coff + 128],
                            tri[:, :].unsqueeze(1).broadcast_to([128, 4, 128]),
                            alu.mult), inc="sv", n=1, mark=("mask", s, i))

                def p3l(i, en):
                    if en["last"]:
                        E.wait("st", ("pv", s, i))
                        g, qb = en["g"], en["qb"]
                        E.dobar(lambda: E.h.tensor_scalar(
                            sums_g[:, :], pD[:, 0:256], 1e-35, None, alu.max))
                        E.waitbar()
                        E.dobar(lambda: E.h.reciprocal(sums_r[:, :],
                                                       sums_g[:, :]))
                        E.waitbar()
                        E.do(lambda g=g, qb=qb: E.h.tensor_tensor(
                            otf[:, g * T + qb * QB: g * T + (qb + 1) * QB],
                            pC[:, 0:256], sums_r[:, :], alu.mult))
                        E.do(lambda: E.h.tensor_copy(pC[:, 0:256], zrow[:, :]))
                        E.do(lambda g=g, qb=qb: E.h.tensor_copy(
                            pD[:, 0:256], zrow[:, :]),
                            inc="sv", n=1, mark=("ot", s, g, qb))


                def vA0(t):
                    E.wait("st", ("womm", s, t, 0))
                    E.do(lambda: E.h.tensor_copy(aosb[:, 0:512], pW[:, :]),
                         inc="sv", n=1, mark=("aoev0", s, t))
                    E.do(lambda: E.h.bn_stats(stats[:, 0:6], aosb[:, 0:512]))

                def vA1(t):
                    d = 60 + 24 * (t % 2)
                    E.wait("st", ("womm", s, t, 1))
                    E.do(lambda: E.h.tensor_copy(aosb[:, 512:1024], pTf[:, :]),
                         inc="sv", n=1, mark=("aoev", s, t))
                    E.do(lambda: E.h.bn_stats(stats[:, 6:12], aosb[:, 512:1024]))
                    combine_ln(E, 0, d, 1.0 / 1024, inc="sv",
                               mark=("ln2stats", s, t))

                def vD(tp):
                    slp = (tp % 2) * 1024
                    E.wait("st", ("atT", s, tp))
                    E.do(lambda slp=slp: E.h.tensor_copy(
                        atT[:, slp:slp + 1024], pT[:, :]),
                        inc="sv", n=1, mark=("atevac", s, tp))

                def vE0(tp):
                    dp = 60 + 24 * (tp % 2)
                    E.wait("st", ("ffnmm", s, tp, 0))
                    E.do(lambda dp=dp: E.h.scalar_tensor_tensor(
                        ffo[:, 0:512], pW[:, :], stats[:, dp + 12:dp + 13],
                        bias_bc[:, 0:512], alu.mult, alu.add),
                        inc="sv", n=1, mark=("ffo0", s, tp))
                    E.do(lambda: E.h.bn_stats(stats[:, 0:6], ffo[:, 0:512]))

                def vE1(tp):
                    dp = 60 + 24 * (tp % 2)
                    E.wait("st", ("ffnmm", s, tp, 1))
                    E.do(lambda dp=dp: E.h.scalar_tensor_tensor(
                        ffo[:, 512:1024], pTf[:, :], stats[:, dp + 12:dp + 13],
                        bias_bc[:, 512:1024], alu.mult, alu.add),
                        inc="sv", n=1, mark=("ffnev", s, tp))
                    E.do(lambda: E.h.bn_stats(stats[:, 6:12], ffo[:, 512:1024]))
                    combine_ln(E, 0, dp + 16, 1.0 / 1024, inc="sv",
                               mark=("ln4stats", s, tp))

                def vB(t):
                    d = 60 + 24 * (t % 2)
                    sl = (t % 2) * 1024
                    xbt = xb[:, sl:sl + 1024]
                    xm = xmid[:, sl:sl + 1024]
                    E.wait("ss", ("rstd2", s, t))
                    E.wait("sb%d" % (t % 2), ("xb", s, t))
                    E.dobar(lambda d=d: E.h.tensor_scalar(
                        stats[:, d + 3:d + 4], stats[:, d:d + 1],
                        stats[:, d + 2:d + 3], -1.0, alu.mult, alu.mult))
                    E.waitbar()
                    E.do(lambda xbt=xbt, d=d: E.h.tensor_scalar(
                        xadj[:, :], xbt, stats[:, d + 3:d + 4], None,
                        alu.add))
                    E.do(lambda sl=sl, d=d: E.h.scalar_tensor_tensor(
                        xmid[:, sl:sl + 1024], aosb[:, :],
                        stats[:, d + 2:d + 3], xadj[:, :], alu.mult,
                        alu.add), inc="sv", n=1, mark=("ln2res", s, t))
                    E.do(lambda xm=xm: E.h.bn_stats(stats[:, 0:6],
                                                    xm[:, 0:512]))
                    E.do(lambda xm=xm: E.h.bn_stats(stats[:, 6:12],
                                                    xm[:, 512:1024]))
                    combine_ln(E, 0, d + 4, 1.0 / 1024)
                    E.do(lambda xm=xm, d=d: E.h.tensor_reduce(
                        stats[:, d + 6:d + 7].unsqueeze(2), xm.unsqueeze(1),
                        AX.X, alu.max))
                    E.do(lambda xm=xm, d=d: E.h.tensor_reduce(
                        stats[:, d + 7:d + 8].unsqueeze(2), xm.unsqueeze(1),
                        AX.X, alu.min), inc="sv", n=1,
                        mark=("ln3stats", s, t))

                def vC(t):
                    d = 60 + 24 * (t % 2)
                    xm = xmid[:, (t % 2) * 1024:(t % 2 + 1) * 1024]
                    E.wait("ss", ("rstd3", s, t))
                    E.do(lambda d=d: E.h.tensor_scalar(
                        stats[:, d + 9:d + 10], stats[:, d + 6:d + 7],
                        stats[:, d + 4:d + 5], None, alu.subtract))
                    E.dobar(lambda d=d: E.h.tensor_scalar(
                        stats[:, d + 10:d + 11], stats[:, d + 4:d + 5],
                        stats[:, d + 7:d + 8], None, alu.subtract))
                    E.waitbar()
                    E.dobar(lambda d=d: E.h.tensor_scalar(
                        stats[:, d + 9:d + 10], stats[:, d + 9:d + 10],
                        stats[:, d + 10:d + 11], None, alu.max))
                    E.waitbar()
                    E.dobar(lambda d=d: E.h.tensor_scalar(
                        stats[:, d + 9:d + 10], stats[:, d + 9:d + 10],
                        stats[:, d + 8:d + 9], 1e-12, alu.mult, alu.max))
                    E.waitbar()
                    E.dobar(lambda d=d: E.h.tensor_scalar(
                        stats[:, d + 11:d + 12], stats[:, d + 9:d + 10],
                        1.0 / 224.0, None, alu.mult))
                    E.waitbar()
                    E.do(lambda d=d: E.h.tensor_scalar(
                        stats[:, d + 10:d + 11], stats[:, d + 11:d + 12],
                        0.0, None, alu.add))
                    E.dobar(lambda d=d: E.h.tensor_scalar(
                        stats[:, d + 12:d + 13], stats[:, d + 11:d + 12],
                        ws128, None, alu.mult))
                    E.waitbar()
                    E.dobar(lambda d=d: E.h.reciprocal(
                        stats[:, d + 13:d + 15], stats[:, d + 10:d + 12]))
                    E.waitbar()
                    E.dobar(lambda d=d: E.h.tensor_scalar(
                        stats[:, d + 13:d + 14], stats[:, d + 8:d + 9],
                        stats[:, d + 14:d + 15], None, alu.mult))
                    E.waitbar()
                    E.dobar(lambda d=d: E.h.tensor_scalar(
                        stats[:, d + 15:d + 16], stats[:, d + 4:d + 5],
                        stats[:, d + 13:d + 14], -1.0, alu.mult, alu.mult))
                    E.waitbar()
                    if t >= 1:
                        E.wait("ss", ("castback", s, t - 1))
                    elif s > 0:
                        E.wait("ss", ("castback", s - 1, NT - 1))
                    E.do(lambda xm=xm, d=d: E.h.tensor_scalar(
                        qin8[:, :], xm, stats[:, d + 13:d + 14],
                        stats[:, d + 15:d + 16], alu.mult, alu.add),
                        inc="sv", n=1, mark=("quant", s, t))

                def vF(tp):
                    dp = 60 + 24 * (tp % 2)
                    slp = (tp % 2) * 1024
                    xmp = xmid[:, slp:slp + 1024]
                    E.wait("ss", ("rstd4", s, tp))
                    E.dobar(lambda dp=dp: E.h.tensor_scalar(
                        stats[:, dp + 19:dp + 20], stats[:, dp + 16:dp + 17],
                        stats[:, dp + 18:dp + 19], -1.0, alu.mult, alu.mult))
                    E.waitbar()
                    E.do(lambda xmp=xmp, dp=dp: E.h.tensor_scalar(
                        xadj[:, :], xmp, stats[:, dp + 19:dp + 20], None,
                        alu.add))
                    if tp >= 2:
                        E.wait("so%d" % (tp % 2), ("out", s, tp - 2))
                    elif s > 0:
                        E.wait("so%d" % (tp % 2), ("out", s - 1, NT + tp - 2))
                    E.do(lambda slp=slp, dp=dp: E.h.scalar_tensor_tensor(
                        xout[:, slp:slp + 1024], ffo[:, :],
                        stats[:, dp + 18:dp + 19], xadj[:, :], alu.mult,
                        alu.add), inc="sv", n=1, mark=("xout", s, tp))

                def p4sched(u0):
                    u1 = u0 + 1
                    out = []

                    def add(slot, fn, u, front):
                        if front and u < NT:
                            out.append((slot, lambda fn=fn, u=u: fn(u)))
                        elif not front and 1 <= u <= NT:
                            out.append((slot, lambda fn=fn, u=u: fn(u - 1)))
                    add(0, vA0, u0, True)
                    add(1, vA1, u0, True)
                    add(2, vD, u0, False)
                    add(3, vE0, u0, False)
                    add(4, vE1, u0, False)
                    add(5, vB, u0, True)
                    add(6, vF, u0, False)
                    add(7, vC, u0, True)
                    add(9, vA0, u1, True)
                    add(10, vA1, u1, True)
                    add(11, vD, u1, False)
                    add(12, vE0, u1, False)
                    add(13, vE1, u1, False)
                    add(14, vB, u1, True)
                    add(15, vF, u1, False)
                    add(16, vC, u1, True)
                    return out

                i = 0
                step = 0
                for qb in range(NQB):
                    pieces = []
                    if qb >= 1:
                        pieces = p4sched(step)
                        step += 2
                    pi = 0
                    k = 0
                    while i < len(ENTRIES) and ENTRIES[i]["qb"] == qb:
                        p3e(i, ENTRIES[i])
                        if i >= 2:
                            p3l(i - 2, ENTRIES[i - 2])
                        i += 1
                        k += 1
                        while pi < len(pieces) and pieces[pi][0] <= k - 2:
                            pieces[pi][1]()
                            pi += 1
                    while pi < len(pieces):
                        pieces[pi][1]()
                        pi += 1
                p3l(len(ENTRIES) - 2, ENTRIES[-2])
                p3l(len(ENTRIES) - 1, ENTRIES[-1])
                for u0 in (NT - 2, NT):
                    for _, fn in p4sched(u0):
                        fn()
                del p3e, p3l, p4sched, vA0, vA1, vB, vC, vD, vE0, vE1, vF

        def stream_scalar(E):
            for j in range(8):
                E.wait("sv", ("decbits", j))
                if j >= 1:
                    E.wait("sv", ("decsel", j - 1))
                E.do(lambda: E.h.activation(dec_A, dec_a.bitcast(FP8),
                                            AF.Copy))
                E.do(lambda: E.h.activation(dec_B, dec_b.bitcast(FP8),
                                            AF.Copy, scale=2.0),
                     inc="ss", n=1, mark=("deccast", j))
            E.wait("sv", "evac_a")
            E.wait("sg", "consts")
            for s in range(SEQS):
                for t in range(NT + 2):
                    if t < NT:
                        d = 16 + 40 * (t % 2)
                        sl = (t % 2) * 1024
                        E.wait("sv", ("ln1stats", s, t))
                        E.do(lambda d=d: E.h.activation(
                            stats[:, d + 2:d + 3], stats[:, d + 1:d + 2],
                            AF.Ln, bias=eps_sb[:, 0:1]))
                        E.do(lambda d=d: E.h.activation(
                            stats[:, d + 2:d + 3], stats[:, d + 2:d + 3],
                            AF.Exp, scale=-0.5),
                             inc="ss", n=1, mark=("rstd1", s, t))
                        E.do(lambda d=d: E.h.activation(
                            stats[:, d + 3:d + 4], stats[:, d + 3:d + 4],
                            AF.Identity, scale=stats[:, d + 2:d + 3]))
                        E.do(lambda d=d, sl=sl: E.h.activation(
                            ln1[:, sl:sl + 1024], xf[:, sl:sl + 1024],
                            AF.Identity, bias=stats[:, d + 3:d + 4],
                            scale=stats[:, d + 2:d + 3]),
                            inc="ss", n=1, mark=("ln1", s, t))
                    if 1 <= t <= NT:
                        tp = t - 1
                        slp = (tp % 2) * 1024
                        trT = pT if tp % 2 == 0 else pT2
                        E.wait("st", ("lnTt", s, tp))
                        E.do(lambda slp=slp, trT=trT: E.h.activation(
                            lnT[:, slp:slp + 1024], trT[:, :], AF.Copy),
                            inc="ss", n=1, mark=("lnTevac", s, tp))
                        E.wait("st", ("kvmm", s, tp))
                        E.do(lambda: E.h.activation(kvT[:, :], pC[:, 256:384],
                                                    AF.Copy),
                             inc="ss", n=1, mark=("kvTevac", s, tp))
                        E.wait("st", ("kvvmm", s, tp))
                    if t >= 2:
                        E.wait("sv", ("rmsstage", s, t - 2))
                        E.do(lambda: E.h.activation(rq[:, :], stats[:, 20:52],
                                                    AF.Ln))
                        E.do(lambda: E.h.activation(rq[:, :], rq[:, :], AF.Exp,
                                                    scale=-0.5),
                             inc="ss", n=1, mark=("rmsr", s, t - 2))
                    if 1 <= t <= NT:
                        if tp >= 1:
                            E.wait("sv", ("kpost", s, tp - 1))
                        elif s > 0:
                            E.wait("sv", ("xout", s - 1, NT - 1))
                        E.do(lambda: E.h.activation(qf[:, :], pA[:, :],
                                                    AF.Copy),
                             inc="ss", n=1, mark=("qfev", s, tp))
                        E.do(lambda: E.h.activation(xadj[:, 0:512], pB[:, 0:512],
                                                    AF.Copy))
                        E.do(lambda tp=tp: E.h.activation(
                            vtok[:, tp * 512:(tp + 1) * 512], pB[:, 512:1024],
                            AF.Copy),
                            inc="ss", n=1, mark=("qkvfree", s, tp))
                def p3x(i, en):
                    slot = i % 4
                    qoff = en["qoff"]
                    st_ = (i % 2) * 1024
                    E.wait("st", ("sc", s, i))
                    E.wait("st", ("pv", s, i - 4))
                    pv = pt_ring[:, slot * 1024:(slot + 1) * 1024]
                    src = pS[:, st_:st_ + 1024].rearrange(
                        "p (h q) -> p h q", h=4)[:, :, qoff:256]
                    dst = pv.rearrange("p (h q) -> p h q", h=4)[:, :, qoff:256] \
                        if qoff else pv.rearrange("p (h q) -> p h q", h=4)
                    E.do(lambda src=src, dst=dst: E.h.activation(
                        dst, src, AF.Exp, bias=ubias),
                        inc="ss", n=1, mark=("exp", s, i))


                def x2(t):
                    d = 60 + 24 * (t % 2)
                    E.wait("sv", ("ln2stats", s, t))
                    E.do(lambda d=d: E.h.activation(
                        stats[:, d + 2:d + 3], stats[:, d + 1:d + 2],
                        AF.Ln, bias=eps_sb[:, 0:1]))
                    E.do(lambda d=d: E.h.activation(
                        stats[:, d + 2:d + 3], stats[:, d + 2:d + 3],
                        AF.Exp, scale=-0.5),
                         inc="ss", n=1, mark=("rstd2", s, t))

                def x4(tp):
                    dp = 60 + 24 * (tp % 2)
                    E.wait("sv", ("ln4stats", s, tp))
                    E.do(lambda dp=dp: E.h.activation(
                        stats[:, dp + 18:dp + 19], stats[:, dp + 17:dp + 18],
                        AF.Ln, bias=eps_sb[:, 0:1]))
                    E.do(lambda dp=dp: E.h.activation(
                        stats[:, dp + 18:dp + 19], stats[:, dp + 18:dp + 19],
                        AF.Exp, scale=-0.5),
                         inc="ss", n=1, mark=("rstd4", s, tp))

                def x3(t):
                    d = 60 + 24 * (t % 2)
                    E.wait("sv", ("ln3stats", s, t))
                    E.do(lambda d=d: E.h.activation(
                        stats[:, d + 8:d + 9], stats[:, d + 5:d + 6],
                        AF.Ln, bias=eps_sb[:, 0:1]))
                    E.do(lambda d=d: E.h.activation(
                        stats[:, d + 8:d + 9], stats[:, d + 8:d + 9],
                        AF.Exp, scale=-0.5),
                         inc="ss", n=1, mark=("rstd3", s, t))

                def xcb(t):
                    E.wait("sv", ("quant", s, t))
                    E.wait("st", ("atT", s, t - 2))
                    E.do(lambda t=t: E.h.activation(
                        abf[:, (t % 2) * 1024:(t % 2 + 1) * 1024], qin8[:, :],
                        AF.Copy), inc="ss", n=1, mark=("castback", s, t))

                def p4sched(u0):
                    u1 = u0 + 1
                    out = []

                    def add(slot, fn, u, front):
                        if front and u < NT:
                            out.append((slot, lambda fn=fn, u=u: fn(u)))
                        elif not front and 1 <= u <= NT:
                            out.append((slot, lambda fn=fn, u=u: fn(u - 1)))
                    add(2, x2, u0, True)
                    add(5, x4, u0, False)
                    add(6, x3, u0, True)
                    add(8, xcb, u0, True)
                    add(11, x2, u1, True)
                    add(14, x4, u1, False)
                    add(15, x3, u1, True)
                    add(17, xcb, u1, True)
                    return out

                i = 0
                step = 0
                for qb in range(NQB):
                    pieces = []
                    if qb >= 1:
                        pieces = p4sched(step)
                        step += 2
                    pi = 0
                    k = 0
                    while i < len(ENTRIES) and ENTRIES[i]["qb"] == qb:
                        p3x(i, ENTRIES[i])
                        i += 1
                        k += 1
                        while pi < len(pieces) and pieces[pi][0] <= k - 4:
                            pieces[pi][1]()
                            pi += 1
                    while pi < len(pieces):
                        pieces[pi][1]()
                        pi += 1
                for u0 in (NT - 2, NT):
                    for _, fn in p4sched(u0):
                        fn()
                del p3x, p4sched, x2, x3, x4, xcb

        def stream_tensor(E):
            E.wait("sv", "usmall")
            E.do(lambda: E.h.matmul(pC[:, 0:1], onesrow[:, :], sm_sb[0:1, 3:4],
                                    start=True, stop=True, tile_position=(0, 0)))
            E.do(lambda: E.h.matmul(pC[:, 1:2], onesrow[:, :], ws_sb[0:1, 0:1],
                                    start=True, stop=True, tile_position=(0, 0)),
                 inc="st", n=1, mark="rank1a")
            E.wait("sv", "evac_a")
            E.do(lambda: E.h.matmul(pC[:, 0:512], onesrow[:, :],
                                    rowbuf[0:1, 0:512], start=True, stop=True,
                                    tile_position=(0, 0)),
                 inc="st", n=1, mark="rank1b")
            E.wait("sv", "evac_b")
            E.do(lambda: E.h.matmul(pC[:, 0:512], onesrow[:, :],
                                    rowbuf[0:1, 512:1024], start=True, stop=True,
                                    tile_position=(0, 0)),
                 inc="st", n=1, mark="rank1c")
            E.wait("ssm", "smalls")
            E.do(lambda: E.h.matmul(pA[:, 0:512], onesrow[:, :], fb_sb[0:1, 0:512],
                                    start=True, stop=True, tile_position=(0, 0)))
            E.do(lambda: E.h.matmul(pA[:, 512:1024], onesrow[:, :],
                                    fb_sb[0:1, 512:1024], start=True, stop=True,
                                    tile_position=(0, 0)),
                 inc="st", n=1, mark="rank1d")
            E.wait("sg", "consts")
            for j in range(8):
                E.wait("sv", ("decsel", j))
                if j >= 1:
                    E.wait("sv", ("wtevac", j - 1))
                for g in range(8):
                    E.do(lambda j=j, g=g: E.h.transpose(
                        pT[:, g * 128:(g + 1) * 128],
                        wn_bf[:, (j % 2) * 1024 + g * 128:(j % 2) * 1024 + (g + 1) * 128],
                        ident[:, :]),
                        inc=("st" if g == 7 else None), n=1,
                        mark=(("wtT", j) if g == 7 else None))

            for s in range(SEQS):

                def p2T(t):
                    sl = (t % 2) * 1024
                    trT = pT if t % 2 == 0 else pT2
                    E.wait("ss", ("ln1", s, t))
                    if t >= 2:
                        E.wait("ss", ("lnTevac", s, t - 2))
                    elif s > 0:
                        E.wait("sv", ("ffnev", s - 1, NT - 1) if t == 0
                               else ("ffo0", s - 1, NT - 1))
                    elif t == 0:
                        E.wait("sv", ("wtevac", 7))
                    for g in range(8):
                        E.do(lambda sl=sl, g=g, trT=trT: E.h.transpose(
                            trT[:, g * 128:(g + 1) * 128],
                            ln1[:, sl + g * 128:sl + (g + 1) * 128], ident[:, :]),
                            inc=("st" if g == 7 else None), n=1,
                            mark=(("lnTt", s, t) if g == 7 else None))

                def p2B1(t):
                    sl = (t % 2) * 1024
                    E.wait("ss", ("lnTevac", s, t))
                    if t > 0:
                        E.wait("ss", ("kvTevac", s, t - 1))
                    elif s > 0:
                        E.wait("ss", ("exp", s - 1, len(ENTRIES) - 1))
                        E.wait("sv", ("ot", s - 1, LAST_G, LAST_QB))
                    else:
                        E.wait("sv", "evac_d")
                    for g in range(8):
                        lhs = lnT[:, sl + g * 128:sl + (g + 1) * 128]
                        E.do(lambda g=g, lhs=lhs: E.h.matmul(
                            pC[:, 256:384], wkv_bf[:, g * 128:(g + 1) * 128], lhs,
                            start=(g == 0), stop=(g == 7)),
                            inc=("st" if g == 7 else None), n=1,
                            mark=(("kvmm", s, t) if g == 7 else None))
                    if t > 0:
                        E.wait("ss", ("qkvfree", s, t - 1))
                    for g in range(8):
                        lhs = lnT[:, sl + g * 128:sl + (g + 1) * 128]
                        E.do(lambda g=g, lhs=lhs: E.h.matmul(
                            pA[:, 0:512], lhs, wq_bf[:, g * 1024:g * 1024 + 512],
                            start=(g == 0), stop=(g == 7)))
                        E.do(lambda g=g, lhs=lhs: E.h.matmul(
                            pA[:, 512:1024], lhs,
                            wq_bf[:, g * 1024 + 512:(g + 1) * 1024],
                            start=(g == 0), stop=(g == 7)))

                def p2B2(t):
                    E.wait("ss", ("kvTevac", s, t))
                    E.do(lambda: E.h.matmul(pB[:, 0:512], kvT[:, :], wk_bf[:, :],
                                            start=True, stop=True))
                    E.do(lambda: E.h.matmul(pB[:, 512:1024], kvT[:, :], wv_bf[:, :],
                                            start=True, stop=True),
                         inc="st", n=1, mark=("kvvmm", s, t))

                def p2Q(t):
                    E.wait("sv", ("qpost", s, t))
                    E.wait("sv", ("kpost", s, t))
                    if t >= 1:
                        E.wait("sv", ("qkTevac", s, t - 1))
                    for j in range(4):
                        E.do(lambda j=j: E.h.transpose(
                            pDb[:, j * 128:(j + 1) * 128],
                            qhat[:, j * 128:(j + 1) * 128], ident[:, :]))
                        E.do(lambda j=j: E.h.transpose(
                            pDb[:, 512 + j * 128:512 + (j + 1) * 128],
                            khat[:, j * 128:(j + 1) * 128], ident[:, :]),
                            inc=("st" if j == 3 else None), n=1,
                            mark=(("qkT", s, t) if j == 3 else None))

                for u in range(NT + 2):
                    if u < NT:
                        p2T(u)
                    if 1 <= u <= NT:
                        p2B1(u - 1)
                        p2B2(u - 1)
                    if 2 <= u <= NT + 1:
                        p2Q(u - 2)
                del p2T, p2B1, p2B2, p2Q

                E.wait("sv", ("qkTevac", s, NT - 1))

                def emit_scores(i, en):
                    g, qb, c, qoff = en["g"], en["qb"], en["c"], en["qoff"]
                    st_ = (i % 2) * 1024
                    if i >= 2:
                        E.wait("ss", ("exp", s, i - 2))
                    for hp in (0, 2, 1, 3):
                        lhs = khT[32 * hp:32 * (hp + 1),
                                  g * T + c * 128: g * T + (c + 1) * 128]
                        rhs = qhT[32 * hp:32 * (hp + 1),
                                  g * T + qb * QB + qoff: g * T + (qb + 1) * QB]
                        dst = pS[:, st_ + hp * 256 + qoff:
                                 st_ + (hp + 1) * 256]
                        if hp in (0, 2):
                            E.do(lambda lhs=lhs, rhs=rhs, dst=dst, hp=hp:
                                 E.h.matmul(
                                     dst, lhs, rhs, start=True, stop=True,
                                     tile_position=(32 * hp, 0),
                                     skip_group_check=True),
                                 inc="stb", n=1,
                                 mark=(("scp", s, i) if hp == 2 else None))
                        else:
                            if hp == 1:
                                E.wait("stb", ("scp", s, i))
                            E.do(lambda lhs=lhs, rhs=rhs, dst=dst, hp=hp:
                                 E.h.matmul(
                                     dst, lhs, rhs, start=True, stop=True,
                                     tile_position=(32 * hp, 0),
                                     skip_group_check=True),
                                 inc=("st" if hp == 3 else None), n=1,
                                 mark=(("sc", s, i) if hp == 3 else None))

                def emit_pv(i, en):
                    g, qb, c, qoff = en["g"], en["qb"], en["c"], en["qoff"]
                    slot = i % 4
                    ri = qb * NG + g
                    roff = 0
                    E.wait("ss", ("exp", s, i))
                    if en["diag"]:
                        E.wait("sv", ("mask", s, i))
                    if en["first"]:
                        if ri >= 1:
                            E.wait("sv", ("ot", s, (ri - 1) % NG,
                                          (ri - 1) // NG))
                        else:
                            E.wait("sv", ("pvz", s))
                    pv = pt_ring[:, slot * 1024:(slot + 1) * 1024]
                    for hp in range(4):
                        h = 4 * g + hp
                        vlhs = vtok[:, c * 512 + 32 * h: c * 512 + 32 * (h + 1)]
                        rhs = pv[:, hp * 256 + qoff:(hp + 1) * 256]
                        E.do(lambda vlhs=vlhs, rhs=rhs, hp=hp, en=en, qoff=qoff,
                             roff=roff: E.h.matmul(
                                 pC[32 * hp:32 * (hp + 1),
                                    roff + qoff:roff + 256], vlhs, rhs,
                                 start=False, stop=en["last"],
                                 tile_position=(0, 32 * hp), skip_group_check=True))
                    for hp in range(4):
                        rhs = pv[:, hp * 256 + qoff:(hp + 1) * 256]
                        E.do(lambda rhs=rhs, hp=hp, en=en, qoff=qoff, roff=roff:
                             E.h.matmul(
                                 pD[32 * hp:32 * (hp + 1),
                                    roff + qoff:roff + 256],
                                 ones32[:, :], rhs, start=False, stop=en["last"],
                                 tile_position=(0, 32 * hp), skip_group_check=True),
                             inc=("st" if hp == 3 else None), n=1,
                             mark=(("pv", s, i) if hp == 3 else None))

                def tW0(t):
                    E.wait("sv", ("ot", s, NG - 1, t // 2))
                    if t >= 2:
                        E.wait("sv", ("ffo0", s, t - 2))
                    elif t == 1:
                        E.wait("sv", ("aoev0", s, 0))
                    else:
                        E.wait("sv", ("qkTevac", s, NT - 1))
                    for g in range(4):
                        lhs = otf[:, g * T + t * 128: g * T + (t + 1) * 128]
                        E.do(lambda g=g, lhs=lhs: E.h.matmul(
                            pW[:, 0:512], lhs,
                            wo_bf[:, g * 1024:g * 1024 + 512],
                            start=(g == 0), stop=(g == 3)),
                            inc=("st" if g == 3 else None), n=1,
                            mark=(("womm", s, t, 0) if g == 3 else None))

                def tW1(t):
                    if t >= 2:
                        E.wait("sv", ("ffnev", s, t - 2))
                    elif t == 1:
                        E.wait("sv", ("aoev", s, 0))
                    else:
                        E.wait("sv", ("qkTevac", s, NT - 2))
                    for g in range(4):
                        lhs = otf[:, g * T + t * 128: g * T + (t + 1) * 128]
                        E.do(lambda g=g, lhs=lhs: E.h.matmul(
                            pTf[:, 0:512], lhs,
                            wo_bf[:, g * 1024 + 512:(g + 1) * 1024],
                            start=(g == 0), stop=(g == 3)),
                            inc=("st" if g == 3 else None), n=1,
                            mark=(("womm", s, t, 1) if g == 3 else None))

                def tAT(tp):
                    slp = (tp % 2) * 1024
                    E.wait("ss", ("castback", s, tp))
                    E.wait("sv", ("aoev", s, tp + 1) if tp + 1 < NT else
                           ("ffnev", s, tp - 1))
                    for g in range(8):
                        E.do(lambda slp=slp, g=g: E.h.transpose(
                            pT[:, g * 128:(g + 1) * 128],
                            abf[:, slp + g * 128:slp + (g + 1) * 128],
                            ident[:, :]),
                            inc=("st" if g == 7 else None), n=1,
                            mark=(("atT", s, tp) if g == 7 else None))

                def tF0(tp):
                    E.wait("sv", ("atevac", s, tp))
                    E.wait("sv", ("aoev0", s, tp + 1) if tp + 1 < NT else
                           ("ffo0", s, tp - 1))
                    slp = (tp % 2) * 1024
                    for g in range(8):
                        lhs = atT[:, slp + g * 128:slp + (g + 1) * 128]
                        E.do(lambda g=g, lhs=lhs: E.h.matmul(
                            pW[:, 0:512], lhs,
                            wt_bf[:, g * 1024:g * 1024 + 512],
                            start=(g == 0), stop=(g == 7)),
                            inc=("st" if g == 7 else None), n=1,
                            mark=(("ffnmm", s, tp, 0) if g == 7 else None))

                def tF1(tp):
                    slp = (tp % 2) * 1024
                    for g in range(8):
                        lhs = atT[:, slp + g * 128:slp + (g + 1) * 128]
                        E.do(lambda g=g, lhs=lhs: E.h.matmul(
                            pTf[:, 0:512], lhs,
                            wt_bf[:, g * 1024 + 512:(g + 1) * 1024],
                            start=(g == 0), stop=(g == 7)),
                            inc=("st" if g == 7 else None), n=1,
                            mark=(("ffnmm", s, tp, 1) if g == 7 else None))

                def p4sched(u0):
                    u1 = u0 + 1
                    out = []

                    def add(slot, fn, u, front):
                        if front and u < NT:
                            out.append((slot, lambda fn=fn, u=u: fn(u)))
                        elif not front and 1 <= u <= NT:
                            out.append((slot, lambda fn=fn, u=u: fn(u - 1)))
                    add(0, tW0, u0, True)
                    add(1, tW1, u0, True)
                    add(2, tAT, u0, False)
                    add(3, tF0, u0, False)
                    add(4, tF1, u0, False)
                    add(9, tW0, u1, True)
                    add(10, tW1, u1, True)
                    add(11, tAT, u1, False)
                    add(12, tF0, u1, False)
                    add(13, tF1, u1, False)
                    return out

                i = 0
                step = 0
                for qb in range(NQB):
                    pieces = []
                    if qb >= 1:
                        pieces = p4sched(step)
                        step += 2
                    pi = 0
                    k = 0
                    while i < len(ENTRIES) and ENTRIES[i]["qb"] == qb:
                        emit_scores(i, ENTRIES[i])
                        if i >= 2:
                            emit_pv(i - 2, ENTRIES[i - 2])
                        i += 1
                        k += 1
                        while pi < len(pieces) and pieces[pi][0] <= k - 2:
                            pieces[pi][1]()
                            pi += 1
                    while pi < len(pieces):
                        pieces[pi][1]()
                        pi += 1
                emit_pv(len(ENTRIES) - 2, ENTRIES[-2])
                emit_pv(len(ENTRIES) - 1, ENTRIES[-1])
                for u0 in (NT - 2, NT):
                    for _, fn in p4sched(u0):
                        fn()
                del emit_scores, emit_pv, p4sched, tW0, tW1, tAT, tF0, tF1

        streams = dict(sync=stream_sync, gpsimd=stream_gpsimd,
                       vector=stream_vector, scalar=stream_scalar,
                       tensor=stream_tensor)
        for name, fn in streams.items():
            fn(Eng(sched, None, sems, dry=True))
        block.sync(lambda eng: streams["sync"](Eng(sched, eng, sems, dry=False)))
        block.gpsimd(lambda eng: streams["gpsimd"](Eng(sched, eng, sems, dry=False)))
        block.vector(lambda eng: streams["vector"](Eng(sched, eng, sems, dry=False)))
        block.scalar(lambda eng: streams["scalar"](Eng(sched, eng, sems, dry=False)))
        block.tensor(lambda eng: streams["tensor"](Eng(sched, eng, sems, dry=False)))

    return nc


_NC_CACHE = None


def kernel(L0_attn_x, L0_attn_Wq, L0_attn_Wkv, L0_attn_Wk, L0_attn_Wv,
           L0_attn_Wo, L0_attn_q_g, L0_attn_k_g, L0_ffn_weight,
           L0_ffn_weight_scale, L0_ffn_bias):
    global _NC_CACHE
    x = np.ascontiguousarray(np.asarray(L0_attn_x, dtype=np.float32))
    common = {
        "wq": np.ascontiguousarray(np.asarray(L0_attn_Wq, dtype=np.float32)),
        "wkv": np.ascontiguousarray(np.asarray(L0_attn_Wkv, dtype=np.float32)),
        "wk": np.ascontiguousarray(np.asarray(L0_attn_Wk, dtype=np.float32)),
        "wv": np.ascontiguousarray(np.asarray(L0_attn_Wv, dtype=np.float32)),
        "wo": np.ascontiguousarray(np.asarray(L0_attn_Wo, dtype=np.float32)),
        "qg": np.asarray(L0_attn_q_g, dtype=np.float32).reshape(1, HD),
        "kg": np.asarray(L0_attn_k_g, dtype=np.float32).reshape(1, ROPE),
        "w8": np.ascontiguousarray(np.asarray(L0_ffn_weight).view(np.uint8)),
        "ws": np.asarray(L0_ffn_weight_scale, dtype=np.float32).reshape(1, 1),
        "fb": np.asarray(L0_ffn_bias, dtype=np.float32).reshape(1, D),
    }
    if _NC_CACHE is None:
        _NC_CACHE = build_core()
    in_maps = []
    for c in range(NCORES):
        m = dict(common)
        m["x"] = x[c * SEQS:(c + 1) * SEQS]
        in_maps.append(m)
    res = run_bass_kernel_spmd(_NC_CACHE, in_maps, core_ids=list(range(NCORES)))
    return np.concatenate([r["out"] for r in res.results], axis=0)
